# version 15
# speedup vs baseline: 1.1915x; 1.0000x over previous
import math
from contextlib import ExitStack, contextmanager
import numpy as np
import concourse.bass as bass
import concourse.mybir as mybir
from concourse.bass_utils import run_bass_kernel_spmd



_GSEQ = [0]


class Tok:
    __slots__ = ("q", "i", "used", "val")

    def __init__(self, q, i):
        self.q = q
        self.i = i
        self.used = False
        self.val = None


class _Op:
    __slots__ = ("seq", "waits", "fn", "sem", "inc", "tok")

    def __init__(self, seq, waits, fn, sem, inc, tok):
        self.seq = seq
        self.waits = waits
        self.fn = fn
        self.sem = sem
        self.inc = inc
        self.tok = tok


class _Q:
    def __init__(self, name, sems):
        self.name = name
        self.sems = sems
        self.sem = sems[0]
        self.ops = []
        self.cur = 0
        self.counts = [0] * len(sems)
        self.group_open = False
        self._barrier_deps = None

    def _mark(self, deps):
        out = []
        if self._barrier_deps:
            bd, self._barrier_deps = self._barrier_deps, None
            for d in bd:
                out.extend(self._mark([d]))
        for d in deps:
            if d is None:
                continue
            if isinstance(d, Tok):
                d.used = True
                out.append(d)
            else:
                sem, val = d
                if val > 0:
                    out.append(d)
        return out

    def run(self, method, *args, deps=(), **kwargs):
        waits = self._mark(deps)

        def fn(e):
            return getattr(e, method)(*args, **kwargs)
        tok = Tok(self, len(self.ops))
        _GSEQ[0] += 1
        self.ops.append(_Op(_GSEQ[0], waits, fn, None, 1, tok))
        return tok

    def raw(self, fn, deps=()):
        waits = self._mark(deps)
        tok = Tok(self, len(self.ops))
        _GSEQ[0] += 1
        self.ops.append(_Op(_GSEQ[0], waits, fn, None, 1, tok))
        return tok

    def tok(self):
        for op in reversed(self.ops):
            if op.tok is not None:
                op.tok.used = True
                return op.tok
        return None

    def dma(self, out, in_, deps=(), **kwargs):
        if not self.group_open:
            sem = self.sems[self.cur]
            if self.counts[self.cur] > 0:
                self._barrier_deps = (self._barrier_deps or []) + \
                    [(sem, self.counts[self.cur])]
            self.group_open = True
        waits = self._mark(deps)

        def fn(e):
            return e.dma_start(out=out, in_=in_, **kwargs)
        sem = self.sems[self.cur]
        self.counts[self.cur] += 16
        _GSEQ[0] += 1
        self.ops.append(_Op(_GSEQ[0], waits, fn, sem, 16, None))

    def gtok(self):
        assert self.group_open
        tok = (self.sems[self.cur], self.counts[self.cur])
        self.cur = (self.cur + 1) % len(self.sems)
        self.group_open = False
        return tok

    def finalize(self):
        cnt = 0
        for op in self.ops:
            if op.tok is not None:
                if op.tok.used:
                    cnt += 1
                    op.sem = self.sem
                    op.inc = 1
                    op.tok.val = cnt
                else:
                    op.sem = None


def _emit(e, *queues, final_waits=()):
    ops = sorted([op for q in queues for op in q.ops], key=lambda o: o.seq)
    waited = {}

    def do_wait(sem, val):
        key = id(sem)
        if waited.get(key, 0) >= val:
            return
        waited[key] = val
        e.wait_ge(sem, val)

    for op in ops:
        for w in op.waits:
            if isinstance(w, Tok):
                assert w.val is not None, "wait on un-finalized token"
                do_wait(w.q.sem, w.val)
            else:
                do_wait(w[0], w[1])
        ins = op.fn(e)
        if op.sem is not None:
            ins.then_inc(op.sem, op.inc)
    for w in final_waits:
        if isinstance(w, Tok):
            do_wait(w.q.sem, w.val)
        else:
            do_wait(w[0], w[1])


class Prog:
    def __init__(self, nc, es, dma_pool=10):
        self.nc = nc
        _GSEQ[0] = 0
        mk = lambda name: es.enter_context(nc.semaphore(name))
        self.v = _Q("v", [mk("q_v")])
        self.s = _Q("s", [mk("q_s")])
        self.t = _Q("t", [mk("q_t")])
        self.g = _Q("g", [mk("q_g")])
        self.io = _Q("io", [mk(f"q_io{i}") for i in range(dma_pool)])
        self.ios = _Q("ios", [mk(f"q_is{i}") for i in range(dma_pool)])
        self.iog = _Q("iog", [mk(f"q_ig{i}") for i in range(dma_pool)])
        self.cvt = _Q("cvt", [mk(f"q_cv{i}") for i in range(4)])

    def queues(self):
        return [self.v, self.s, self.t, self.g, self.io, self.ios, self.iog,
                self.cvt]

    def barrier_queues(self):
        return [q for q in self.queues() if q.name != "cvt"]

    def barrier(self):
        toks = []
        for q in self.barrier_queues():
            if len(q.sems) > 1:
                assert not q.group_open, "close dma group before barrier"
                for sem, cnt in zip(q.sems, q.counts):
                    if cnt > 0:
                        toks.append((sem, cnt))
            else:
                t = q.tok()
                if t is not None:
                    toks.append(t)
        for q in self.barrier_queues():
            bd = [t for t in toks
                  if not (isinstance(t, Tok) and t.q is q)]
            q._barrier_deps = (q._barrier_deps or []) + bd

    def emit(self, block, final_waits=()):
        for q in self.queues():
            q.finalize()

        @block.sync
        def _(e):
            _emit(e, self.io)

        @block.vector
        def _(e):
            _emit(e, self.v)

        @block.scalar
        def _(e):
            _emit(e, self.ios, self.s)

        @block.tensor
        def _(e):
            _emit(e, self.t)

        @block.gpsimd
        def _(e):
            _emit(e, self.iog, self.g, self.cvt, final_waits=final_waits)




F32 = mybir.dt.float32
BF16 = mybir.dt.bfloat16
AF = mybir.ActivationFunctionType
ALU = mybir.AluOpType
AXX = mybir.AxisListType.X


def ln_apply(mk, x_tile_sb, out_ap, D, eps, deps):
    v, s = mk.pr.v, mk.pr.s
    if getattr(mk, "_ln_scope", None) is not mk.es_local:
        mk._ln_scope = mk.es_local
        mk._ln_tiles = [mk.sb([128, 1]) for _ in range(6)]
        mk._ln_prev = {}
    mu, ssq, var, rstd, mu2, sq = mk._ln_tiles
    lp = mk._ln_prev
    t1 = v.run("reduce_sum", mu[:], x_tile_sb[:], axis=AXX, deps=deps)
    t2 = s.run("activation", mk._scrap[:, 0:D], x_tile_sb[:], AF.Square,
               accum_out=ssq[:],
               deps=list(deps) + [mk._scrap_free, lp.get("t4")])
    mk._scrap_free = t2
    t3 = v.run("tensor_scalar", mu[:], mu[:], 1.0 / D, None, ALU.mult,
               deps=[t1])
    t4 = v.run("tensor_scalar", var[:], ssq[:], 1.0 / D, eps, ALU.mult,
               ALU.add, deps=[t2, lp.get("t7")])
    t5 = v.run("tensor_tensor", mu2[:], mu[:], mu[:], ALU.mult, deps=[t3])
    t6 = v.run("tensor_tensor", var[:], var[:], mu2[:], ALU.subtract,
               deps=[t4, t5])
    t7 = s.run("activation", sq[:], var[:], AF.Ln, deps=[t6])
    t8 = s.run("activation", rstd[:], sq[:], AF.Exp, scale=mk.const(-0.5),
               deps=[t7, mk.ctok(-0.5), lp.get("t9")])
    t9 = v.run("tensor_scalar", out_ap, x_tile_sb[:], mu[:, 0:1],
               rstd[:, 0:1], ALU.subtract, ALU.mult, deps=[t8, t3])
    mk._ln_prev = dict(t4=t4, t7=t7, t9=t9)
    return t9


def convert_ffn_weights(mk):
    cfg = mk.cfg
    pr = mk.pr
    iog = pr.cvt
    D, HF, E = cfg["D"], cfg["HF"], cfg["E"]
    DC, HC = D // 128, HF // 128
    Wg_s, Wu_s, Wd_s, Wg_e, Wu_e, Wd_e = mk.ffn_w
    from contextlib import ExitStack
    mk.cvt_es = ExitStack()
    old_es = mk.es_local
    mk.es_local = mk.cvt_es
    h0 = (HC + 1) // 2
    stg = [mk.sb([128, max(h0 * D, DC * h0 * 128)], BF16) for _ in range(2)]
    mk.es_local = old_es
    stg_free = [None, None]
    jobs = []
    for ei in range(E + 1):
        ws = (Wg_s, Wu_s, Wd_s) if ei == 0 else \
            (Wg_e[ei - 1], Wu_e[ei - 1], Wd_e[ei - 1])
        for kind, W in zip("gud", ws):
            jobs += [(kind, ei, W, 0), (kind, ei, W, 1)]
    for ji, (kind, ei, W, half) in enumerate(jobs):
        b = ji % 2
        lo = 0 if half == 0 else h0
        hi = h0 if half == 0 else HC
        nh = hi - lo
        if kind == "d":
            iog.dma(stg[b][:, 0:nh * D],
                    W.rearrange("(c p) n -> p c n", p=128)[:, lo:hi, :],
                    deps=[stg_free[b]])
        else:
            iog.dma(stg[b][:, 0:DC * nh * 128]
                    .rearrange("p (c n) -> p c n", n=nh * 128),
                    W.rearrange("(c p) n -> p c n", p=128)
                    [:, :, lo * 128:hi * 128],
                    deps=[stg_free[b]])
        rd = iog.gtok()
        if kind == "d":
            iog.dma(mk.wd_bf[ei][:, lo * D:hi * D], stg[b][:, 0:nh * D],
                    deps=[rd])
        else:
            off = 0 if kind == "g" else D
            for hc in range(lo, hi):
                iog.dma(mk.wgu_bf[ei, hc, :, off:off + D]
                        .rearrange("p (c f) -> p c f", f=128),
                        stg[b][:, 0:DC * nh * 128]
                        .rearrange("p (c n) -> p c n", n=nh * 128)
                        [:, :, (hc - lo) * 128:(hc - lo + 1) * 128],
                        deps=[rd] if hc == lo else [])
        stg_free[b] = iog.gtok()
    mk.cvt_done = [stg_free[0], stg_free[1]]


def ln_transpose_stream(mk, src_dram, xT_s, S, D, eps, with_convert=False):
    pr = mk.pr
    v, t, io, iog, ios = pr.v, pr.t, pr.io, pr.iog, pr.ios
    NT, DC = S // 128, D // 128
    with mk.scope():
        xt = [mk.sb([128, D], F32) for _ in range(2)]
        xln = [mk.sb([128, D], BF16) for _ in range(2)]
        stage = [mk.sb([128, D], BF16) for _ in range(2)]
        ps_tr = mk.ps([128, 2048], BF16)
        tr_free = [None, None]
        xt_free = [None, None]
        st_free = [None, None]
        for i in range(NT):
            b = i % 2
            io.dma(xt[b][:], src_dram[i * 128:(i + 1) * 128, :],
                   deps=[xt_free[b]])
            dl = io.gtok()
            tln = ln_apply(mk, xt[b], xln[b][:], D, eps, deps=[dl, xt_free[b]])
            xt_free[b] = tln
            evs = []
            for c in range(DC):
                sl = c % 2
                tt = t.run("transpose", ps_tr[:, sl * 1024:sl * 1024 + 128],
                           xln[b][:, c * 128:(c + 1) * 128], mk.identb_sb[:],
                           deps=[tln, tr_free[sl], st_free[b], mk.identb_tok])
                ev = v.run("tensor_copy", stage[b][:, c * 128:(c + 1) * 128],
                           ps_tr[:, sl * 1024:sl * 1024 + 128], deps=[tt])
                tr_free[sl] = ev
                evs.append(ev)
            ios.dma(xT_s[i], stage[b][:], deps=evs)
            st_free[b] = ios.gtok()


def build_attn_a(mk, x_in, Wqkv, Wo, xT_s, q_s, k_s, v_s, o_s, x1_s):
    cfg = mk.cfg
    S, D, HD, W = cfg["S"], cfg["D"], cfg["HD"], cfg["W"]
    scale_att = 1.0 / math.sqrt(HD)

    stop = getattr(mk, "stop", None)
    ln_transpose_stream(mk, x_in, xT_s, S, D, 1e-5)
    if stop == "xT":
        return
    v_pass(mk, Wqkv, xT_s, v_s)
    if stop == "v":
        return
    qk_pass(mk, Wqkv, 0, mk.qnw_sb, mk.qnw_tok, q_s, xT_s)
    if stop == "q":
        return
    qk_pass(mk, Wqkv, D, mk.knw_sb, mk.knw_tok, k_s, xT_s)
    convert_ffn_weights(mk)
    if stop == "k":
        return
    attention_pass(mk, q_s, k_s, v_s, o_s, scale_att)
    if stop == "attn":
        return
    wo_residual_pass(mk, Wo, o_s, x_in, x1_s)


def v_pass(mk, Wqkv, xT_s, v_s):
    cfg = mk.cfg
    pr = mk.pr
    v, t, io, iog, ios = pr.v, pr.t, pr.io, pr.iog, pr.ios
    S, D = cfg["S"], cfg["D"]
    NT, DC = S // 128, D // 128
    ND4 = max(1, D // 512)
    NW = min(512, D)
    with mk.scope():
        wv = mk.sb([128, DC * D], BF16, "wv")
        iog.dma(wv[:].rearrange("p (c n) -> p c n", n=D),
                Wqkv.rearrange("(c p) m -> p c m", p=128)[:, :, 2 * D:3 * D])
        dwv = iog.gtok()
        xblk = [mk.sb([128, D], BF16) for _ in range(2)]
        vstage = [mk.sb([128, D], BF16) for _ in range(2)]
        ps_v = [mk.ps([128, ND4 * 512], F32) for _ in range(2)]
        xb_free = [None, None]
        vs_free = [None, None]
        psv_free = [None, None]
        for i in range(NT):
            b = i % 2
            io.dma(xblk[b][:], xT_s[i], deps=[xb_free[b]])
            dxb = io.gtok()
            mm = None
            for c in range(DC):
                for n in range(ND4):
                    mm = t.run(
                        "matmul", ps_v[b][:, n * 512:n * 512 + NW],
                        xblk[b][:, c * 128:(c + 1) * 128],
                        wv[:, c * D + n * NW: c * D + (n + 1) * NW],
                        start=(c == 0), stop=(c == DC - 1),
                        deps=([dxb, dwv, vs_free[b], psv_free[b]]
                              if c == 0 and n == 0 else []))
            evs = []
            for n in range(ND4):
                evs.append(v.run("tensor_copy",
                                 vstage[b][:, n * NW:(n + 1) * NW],
                                 ps_v[b][:, n * 512:n * 512 + NW], deps=[mm]))
            xb_free[b] = evs[-1]
            psv_free[b] = evs[-1]
            ios.dma(v_s[i * 128:(i + 1) * 128, :], vstage[b][:], deps=evs)
            vs_free[b] = ios.gtok()


def qk_pass(mk, Wqkv, wcol0, nw_sb, nw_tok, dst, xT_s):
    cfg = mk.cfg
    pr = mk.pr
    v, s, t, io, iog, ios = pr.v, pr.s, pr.t, pr.io, pr.iog, pr.ios
    S, D, NH, HD = cfg["S"], cfg["D"], cfg["NH"], cfg["HD"]
    DC = D // 128
    TB = min(512, S)
    NTT = TB // 128
    NBK = S // TB
    d2 = HD // 2
    eps_rms = 1e-6
    with mk.scope():
        cos_sb = [mk.sb([128, TB], F32) for _ in range(2)]
        sin_sb = [mk.sb([128, TB], F32) for _ in range(2)]
        cs_free = [[None, None], [None, None]]
        wq = mk.sb([128, DC * D], BF16, "wq")
        iog.dma(wq[:].rearrange("p (c n) -> p c n", n=D),
                Wqkv.rearrange("(c p) m -> p c m", p=128)
                [:, :, wcol0:wcol0 + D])
        dwq = iog.gtok()
        xblk = [mk.sb([128, DC * TB], BF16) for _ in range(2)]
        qblk = mk.sb([128, NH * TB], BF16)
        NSQ = min(2, NH)
        q2 = mk.sb([128, NSQ * TB], F32)
        onescol = mk.sb([1, 128], F32)
        onesrow = mk.sb([128, 1], F32)
        c1 = v.run("memset", onescol[:], 1.0)
        c2 = v.run("memset", onesrow[:], 1.0)
        rstd_v = mk.sb([1, TB], F32)
        stage2 = [mk.sb([128, NH * TB], BF16) for _ in range(2)]
        tmpA2 = [mk.sb([64, TB], F32) for _ in range(2)]
        tmpB2 = [mk.sb([64, TB], F32) for _ in range(2)]
        ps_q = [mk.ps([128, 512], F32) for _ in range(2)]
        ps_ss = mk.ps([128, 512], F32)
        ps_bc = mk.ps([128, 512], F32)
        xb_free = [None, None]
        q_free2 = [None, None]
        st_free2 = [None, None]
        ssq_free = None
        bc_free = None
        rt_free = [None, None]
        rope_prev = [None] * NH
        for i in range(NBK):
            b = i % 2
            io.dma(cos_sb[b][:], mk.cos_d[:, i * TB:(i + 1) * TB],
                   deps=[f for f in cs_free[b] if f is not None])
            io.dma(sin_sb[b][:], mk.sin_d[:, i * TB:(i + 1) * TB])
            cs_tok = io.gtok()
            for r in range(NTT):
                io.dma(xblk[b][:].rearrange("p (c f) -> p c f", f=TB)
                       [:, :, r * 128:(r + 1) * 128],
                       xT_s[i * NTT + r].rearrange("p (c f) -> p c f", f=128),
                       deps=[xb_free[b]] if r == 0 else [])
            dxb = io.gtok()
            ev_q, ss_toks, sq_toks = [], [], []
            last_qmm = None
            mm_ss = None
            for h in range(NH):
                hp = h % 2
                sl = h % NSQ
                mm = None
                for c in range(DC):
                    mm = t.run("matmul", ps_q[hp][:, 0:TB],
                               wq[:, c * D + h * HD: c * D + h * HD + HD],
                               xblk[b][:, c * TB:(c + 1) * TB],
                               start=(c == 0), stop=(c == DC - 1),
                               deps=([dxb, dwq] + (q_free2[hp] or [])
                                     if c == 0 else []))
                last_qmm = mm
                ev = v.run("tensor_copy", qblk[:, h * TB:(h + 1) * TB],
                           ps_q[hp][:, 0:TB],
                           deps=[mm, rope_prev[h]])
                sq = s.run("activation", q2[:, sl * TB:(sl + 1) * TB],
                           qblk[:, h * TB:(h + 1) * TB], AF.Square,
                           deps=[ev, ssq_free if h < NSQ
                                 else ss_toks[h - NSQ]])
                ev_q.append(ev)
                sq_toks.append(sq)
                if h >= 1:
                    ph = h - 1
                    sl_p = ph % NSQ
                    mm_ss = t.run("matmul", ps_ss[0:1, 0:TB],
                                  onesrow[:, 0:1],
                                  q2[:, sl_p * TB:(sl_p + 1) * TB],
                                  start=(ph == 0), stop=False,
                                  deps=([sq_toks[ph], c2, ssq_free]
                                        if ph == 0 else [sq_toks[ph]]))
                    ss_toks.append(mm_ss)
                q_free2[hp] = [ev, sq]
            mm_ss = t.run("matmul", ps_ss[0:1, 0:TB], onesrow[:, 0:1],
                          q2[:, (NH - 1) % NSQ * TB:
                             ((NH - 1) % NSQ + 1) * TB],
                          start=False, stop=True, deps=[sq_toks[NH - 1]])
            ss_toks.append(mm_ss)
            t1 = v.run("tensor_scalar", rstd_v[:], ps_ss[0:1, 0:TB], 1.0 / D,
                       eps_rms, ALU.mult, ALU.add, deps=[mm_ss])
            ssq_free = t1
            t2 = s.run("activation", rstd_v[:], rstd_v[:], AF.Ln, deps=[t1])
            t3 = s.run("activation", rstd_v[:], rstd_v[:], AF.Exp,
                       scale=mk.const(-0.5)[0:1, 0:1],
                       deps=[t2, mk.ctok(-0.5)])
            bc = t.run("matmul", ps_bc[:, 0:TB], onescol[0:1, :], rstd_v[:],
                       start=True, stop=True, deps=[t3, c1, bc_free])
            rp_toks = []
            for h in range(NH):
                nq = v.run("scalar_tensor_tensor",
                           qblk[:, h * TB:(h + 1) * TB],
                           qblk[:, h * TB:(h + 1) * TB],
                           nw_sb[:, h:h + 1], ps_bc[:, 0:TB],
                           ALU.mult, ALU.mult,
                           deps=[bc, ev_q[h], nw_tok])
                bc_free = nq
                hp2 = h % 2
                qq = v if hp2 == 0 else pr.g
                tmpA = tmpA2[hp2]
                tmpB = tmpB2[hp2]
                x1a = qblk[0:d2, h * TB:(h + 1) * TB]
                x2a = qblk[d2:HD, h * TB:(h + 1) * TB]
                cos0 = cos_sb[b][0:d2, 0:TB]
                sin0 = sin_sb[b][0:d2, 0:TB]
                cos64 = cos_sb[b][64:64 + d2, 0:TB]
                sin64 = sin_sb[b][64:64 + d2, 0:TB]
                o1 = stage2[b][0:d2, h * TB:(h + 1) * TB]
                o2 = stage2[b][d2:HD, h * TB:(h + 1) * TB]
                a1 = qq.run("tensor_tensor", tmpA[:, 0:TB], x1a, cos0,
                            ALU.mult, deps=[nq, cs_tok, rt_free[hp2]])
                a2 = qq.run("tensor_tensor", tmpB[:, 0:TB], x2a, sin64,
                            ALU.mult, deps=[nq])
                a3 = qq.run("tensor_tensor", o1, tmpA[:, 0:TB],
                            tmpB[:, 0:TB], ALU.subtract,
                            deps=[a1, a2, st_free2[b]])
                b1 = qq.run("tensor_tensor", tmpA[:, 0:TB], x2a, cos64,
                            ALU.mult, deps=[a3])
                b2 = qq.run("tensor_tensor", tmpB[:, 0:TB], x1a, sin0,
                            ALU.mult, deps=[a3])
                b3 = qq.run("tensor_tensor", o2, tmpA[:, 0:TB],
                            tmpB[:, 0:TB], ALU.add, deps=[b1, b2, st_free2[b]])
                rt_free[hp2] = b3
                cs_free[b][hp2] = b3
                rope_prev[h] = b3
                rp_toks.append(b3)
            xb_free[b] = last_qmm
            for r in range(NTT):
                ios.dma(dst[i * NTT + r].rearrange("p (hh f) -> p hh f",
                                                   f=128),
                        stage2[b][:].rearrange("p (hh f) -> p hh f", f=TB)
                        [:, :, r * 128:(r + 1) * 128],
                        deps=rp_toks if r == 0 else [])
            st_free2[b] = ios.gtok()


def attention_pass(mk, q_s, k_s, v_s, o_s, scale_att):
    cfg = mk.cfg
    pr = mk.pr
    v, s, t, io, iog, ios = pr.v, pr.s, pr.t, pr.io, pr.iog, pr.ios
    S, D, NH, HD, W = cfg["S"], cfg["D"], cfg["NH"], cfg["HD"], cfg["W"]
    NT, KW = S // 128, W // 128
    NSLOT = KW + 2
    with mk.scope():
        qt_sb = [mk.sb([128, NH * 128], BF16) for _ in range(2)]
        kring = [mk.sb([128, NH * 128], BF16) for _ in range(NSLOT)]
        vring = [mk.sb([128, D], BF16) for _ in range(NSLOT)]
        P_sb = mk.sb([128, (KW + 1) * 128], BF16)
        PT_sb = mk.sb([128, (KW + 1) * 128], BF16)
        ostage = [mk.sb([128, NH * 128], BF16) for _ in range(2)]
        den = mk.sb([128, 1], F32)
        mxn = mk.sb([128, 1], F32)
        rec = mk.sb([128, 1], F32)
        ps_sc = mk.ps([128, (KW + 1) * 128], F32)
        ps_tr = mk.ps([128, 2048], BF16)
        ps_o = mk.ps([128, 512], F32)
        kv_last = [None] * NSLOT
        q_free = [None, None]
        o_free = [None, None]
        tr2_free = [None, None]
        P_free = None
        sc_free = None
        o_ps_free = None
        csc = mk.const(scale_att)
        csc_tok = mk.ctok(scale_att)
        ev_o = None
        for qt in range(NT):
            b = qt % 2
            sl = qt % NSLOT
            io.dma(qt_sb[b][:], q_s[qt], deps=[q_free[b]])
            io.dma(kring[sl][:], k_s[qt], deps=[kv_last[sl]])
            io.dma(vring[sl][:], v_s[qt * 128:(qt + 1) * 128, :],
                   deps=[kv_last[sl]])
            dload = io.gtok()
            k0 = max(0, qt - KW)
            nk = qt - k0 + 1
            for h in range(NH):
                bank_ops = {}
                for j in range(nk):
                    bank_ops.setdefault(j // 4, []).append(("s", j))
                if nk == KW + 1:
                    bank_ops[0].append(("m0", 0))
                bank_ops[(nk - 1) // 4].append(("m8", nk - 1))
                mm = None
                for bank in sorted(bank_ops):
                    ops = bank_ops[bank]
                    for oi, (kind, j) in enumerate(ops):
                        st_ = (oi == 0)
                        sp_ = (oi == len(ops) - 1)
                        dd = []
                        if bank == 0 and oi == 0:
                            dd = [dload, sc_free]
                        if kind == "s":
                            kt = k0 + j
                            ksl = kt % NSLOT
                            mm = t.run(
                                "matmul", ps_sc[:, j * 128:(j + 1) * 128],
                                qt_sb[b][:, h * 128:(h + 1) * 128],
                                kring[ksl][:, h * 128:(h + 1) * 128],
                                start=st_, stop=sp_, deps=dd)
                            kv_last[ksl] = mm
                        else:
                            msk = mk.m0_sb if kind == "m0" else mk.m8_sb
                            mm = t.run(
                                "matmul", ps_sc[:, j * 128:(j + 1) * 128],
                                mk.identb_sb[:], msk[:],
                                start=st_, stop=sp_,
                                deps=dd + [mk.identb_tok])
                t1 = v.run("reduce_max", mxn[:], ps_sc[:, 0:nk * 128],
                           axis=AXX, deps=[mm])
                t2 = v.run("tensor_scalar", mxn[:], mxn[:], -scale_att, None,
                           ALU.mult, deps=[t1])
                t3 = s.run("activation", P_sb[:, 0:nk * 128],
                           ps_sc[:, 0:nk * 128], AF.Exp,
                           scale=csc, bias=mxn[:, 0:1], accum_out=den[:],
                           deps=[t2, P_free, csc_tok])
                sc_free = t3
                t4 = v.run("reciprocal", rec[:], den[:], deps=[t3])
                t5 = v.run("tensor_scalar", P_sb[:, 0:nk * 128],
                           P_sb[:, 0:nk * 128], rec[:, 0:1], None, ALU.mult,
                           deps=[t4])
                evs = []
                for j in range(nk):
                    sl2 = j % 2
                    tt = t.run("transpose",
                               ps_tr[:, sl2 * 1024:sl2 * 1024 + 128],
                               P_sb[:, j * 128:(j + 1) * 128],
                               mk.identb_sb[:],
                               deps=[t5, tr2_free[sl2]] +
                                    ([P_free] if j == 0 else []))
                    ev = v.run("tensor_copy", PT_sb[:, j * 128:(j + 1) * 128],
                               ps_tr[:, sl2 * 1024:sl2 * 1024 + 128],
                               deps=[tt])
                    tr2_free[sl2] = ev
                    evs.append(ev)
                P_free = evs[-1]
                mm2 = None
                for j in range(nk):
                    kt = k0 + j
                    ksl = kt % NSLOT
                    mm2 = t.run("matmul", ps_o[:, 0:128],
                                vring[ksl][:, h * HD:(h + 1) * HD],
                                PT_sb[:, j * 128:(j + 1) * 128],
                                start=(j == 0), stop=(j == nk - 1),
                                deps=([evs[j], o_ps_free] if j == 0
                                      else [evs[j]]))
                    kv_last[ksl] = mm2
                ev_o = v.run("tensor_copy",
                             ostage[b][:, h * 128:(h + 1) * 128],
                             ps_o[:, 0:128], deps=[mm2, o_free[b]])
                o_ps_free = ev_o
            q_free[b] = ev_o
            ios.dma(o_s[qt], ostage[b][:], deps=[ev_o])
            o_free[b] = ios.gtok()


def wo_residual_pass(mk, Wo, o_s, x_in, x1_s):
    cfg = mk.cfg
    pr = mk.pr
    v, t, io, iog, ios = pr.v, pr.t, pr.io, pr.iog, pr.ios
    S, D = cfg["S"], cfg["D"]
    DC = D // 128
    TB = min(512, S)
    NBK = S // TB
    NTT = TB // 128
    with mk.scope():
        wo = mk.sb([128, DC * D], BF16, "wo")
        iog.dma(wo[:].rearrange("p (c n) -> p c n", n=D),
                Wo.rearrange("(c p) m -> p c m", p=128))
        dwo = iog.gtok()
        oblk = [mk.sb([128, DC * TB], BF16) for _ in range(2)]
        yfm = mk.sb([128, DC * TB], BF16)
        xres = [mk.sb([128, D], F32) for _ in range(NTT)]
        x1t = xres
        ps_y = [mk.ps([128, 512], F32) for _ in range(2)]
        ps_tr = mk.ps([128, 2048], BF16)
        ob_free = [None, None]
        y_free = [None, None]
        tr_free = [None, None]
        x1_free = [None] * NTT
        for i in range(NBK):
            b = i % 2
            for r in range(NTT):
                io.dma(oblk[b][:].rearrange("p (c f) -> p c f", f=TB)
                       [:, :, r * 128:(r + 1) * 128],
                       o_s[i * NTT + r].rearrange("p (c f) -> p c f", f=128),
                       deps=[ob_free[b]] if r == 0 else [])
            for j in range(NTT):
                io.dma(xres[j][:],
                       x_in[i * TB + j * 128: i * TB + (j + 1) * 128, :],
                       deps=[x1_free[j]])
            dob = io.gtok()
            evs = []
            for dc in range(DC):
                dp = dc % 2
                mm = None
                for c in range(DC):
                    mm = t.run("matmul", ps_y[dp][:, 0:TB],
                               wo[:, c * D + dc * 128: c * D + (dc + 1) * 128],
                               oblk[b][:, c * TB:(c + 1) * TB],
                               start=(c == 0), stop=(c == DC - 1),
                               deps=([dob, dwo, y_free[dp]] if c == 0 else []))
                ev = v.run("tensor_copy", yfm[:, dc * TB:(dc + 1) * TB],
                           ps_y[dp][:, 0:TB], deps=[mm])
                y_free[dp] = ev
                evs.append(ev)
            ob_free[b] = evs[-1]
            for j in range(NTT):
                don = []
                for dc in range(DC):
                    sl = dc % 2
                    tt = t.run("transpose",
                               ps_tr[:, sl * 1024:sl * 1024 + 128],
                               yfm[:, dc * TB + j * 128:
                                   dc * TB + (j + 1) * 128],
                               mk.identb_sb[:], deps=[evs[dc], tr_free[sl]])
                    ad = v.run("tensor_tensor",
                               x1t[j][:, dc * 128:(dc + 1) * 128],
                               ps_tr[:, sl * 1024:sl * 1024 + 128],
                               xres[j][:, dc * 128:(dc + 1) * 128], ALU.add,
                               deps=[tt, dob, x1_free[j]])
                    tr_free[sl] = ad
                    don.append(ad)
                ios.dma(x1_s[i * TB + j * 128: i * TB + (j + 1) * 128, :],
                        x1t[j][:], deps=don)
                x1_free[j] = ios.gtok()




F32 = mybir.dt.float32
BF16 = mybir.dt.bfloat16
AF = mybir.ActivationFunctionType
ALU = mybir.AluOpType
AXX = mybir.AxisListType.X


def build_ffn(mk, src_dram, dst_dram, T, Wg_s, Wu_s, Wd_s, Wg_e, Wu_e, Wd_e,
              Wr, br_d, rb_d, final):
    cfg = mk.cfg
    pr = mk.pr
    v, s, t, io, iog, ios = pr.v, pr.s, pr.t, pr.io, pr.iog, pr.ios
    D, HF, E = cfg["D"], cfg["HF"], cfg["E"]
    DC, HC = D // 128, HF // 128
    TB = min(1024, T)
    NBK = T // TB
    NTT = TB // 128
    ND4 = max(1, D // 512)
    NW = min(512, D)
    TW = min(512, TB)
    NC5 = TB // TW
    tpc = TW // 128

    if getattr(mk, "cvt_es", None) is not None:
        for q in pr.barrier_queues():
            q._barrier_deps = (q._barrier_deps or []) +                 list(getattr(mk, "cvt_done", []))
        mk.cvt_es.close()
        mk.cvt_es = None
    with mk.scope():
        wr_sb = mk.sb([128, DC * E], BF16)
        iog.dma(wr_sb[:].rearrange("p (c n) -> p c n", n=E),
                Wr.rearrange("(c p) m -> p c m", p=128))
        dwr = iog.gtok()
        brrow = mk.sb([1, E], F32)
        rbrow = mk.sb([1, E], F32)
        io.dma(brrow[:], br_d)
        io.dma(rbrow[:], rb_d)
        dbr = io.gtok()
        onescol = mk.sb([1, 128], F32)
        c1 = v.run("memset", onescol[:], 1.0)
        bias_r = mk.sb([1, E], F32)
        tb1 = v.run("tensor_tensor", bias_r[:], brrow[:], rbrow[:], ALU.add,
                    deps=[dbr])
        ps_y = [mk.ps([128, 512], F32) for _ in range(2)]
        ps_r = ps_y[0]
        bcmm = t.run("matmul", ps_r[:, 0:E], onescol[0:1, :], bias_r[:],
                     start=True, stop=True, deps=[tb1, c1])
        bias_bc = mk.sb([128, E], F32)
        tb2 = v.run("tensor_copy", bias_bc[:], ps_r[:, 0:E], deps=[bcmm])
        ones8 = mk.sb([128, E], F32)
        c8 = v.run("memset", ones8[:], 1.0)

        x2acc = [mk.sb([128, D], F32) for _ in range(NTT)]
        xln = mk.sb([128, D], BF16)
        xLT = mk.sb([128, DC * TB], BF16)
        h_e = mk.sb([128, HC * TB], BF16)
        w_rt = [mk.sb([128, E], F32) for _ in range(NTT)]
        sg2 = [mk.sb([128, 512], BF16) for _ in range(2)]
        wgu = [mk.sb([128, 2 * D], BF16) for _ in range(2)]
        wd = [mk.sb([128, HC * D], BF16)]
        ps_tr = mk.ps([128, 2048], BF16)
        ps_g = [mk.ps([128, 512], F32) for _ in range(2)]
        ps_u = [mk.ps([128, 512], F32) for _ in range(2)]

        x2_free = [None] * NTT
        xln_free = None
        xLT_free = None
        h_free = None
        wgu_free = [None, None]
        wd_free = [None, None]
        tr_free = [None, None]
        sg_free = [None, None]
        yps_free = [None, None]
        gps_free = [None, None]
        ups_free = [None, None]
        rps_free = None
        w_free = [None] * NTT

        wsrc = [(None, Wg_s, Wu_s, Wd_s)] + [
            (e, Wg_e, Wu_e, Wd_e) for e in range(E)]

        for i in range(NBK):
            ln_toks = []
            for j in range(NTT):
                io.dma(x2acc[j][:],
                       src_dram[i * TB + j * 128: i * TB + (j + 1) * 128, :],
                       deps=[x2_free[j]])
                dx = io.gtok()
                tln = ln_apply(mk, x2acc[j], xln[:], D, 1e-5,
                               deps=[dx, xln_free])
                evs = []
                for c in range(DC):
                    sl = c % 2
                    tt = t.run("transpose",
                               ps_tr[:, sl * 1024:sl * 1024 + 128],
                               xln[:, c * 128:(c + 1) * 128], mk.identb_sb[:],
                               deps=[tln, tr_free[sl], xLT_free,
                                     mk.identb_tok])
                    ev = v.run("tensor_copy",
                               xLT[:, c * TB + j * 128: c * TB + (j + 1) * 128],
                               ps_tr[:, sl * 1024:sl * 1024 + 128], deps=[tt])
                    tr_free[sl] = ev
                    evs.append(ev)
                xln_free = evs[-1]
                ln_toks.append(evs)
                mm = None
                for c in range(DC):
                    mm = t.run("matmul", ps_r[:, 0:E],
                               xLT[:, c * TB + j * 128: c * TB + (j + 1) * 128],
                               wr_sb[:, c * E:(c + 1) * E],
                               start=(c == 0), stop=(c == DC - 1),
                               deps=[evs[c]] + ([dwr, rps_free,
                                                 yps_free[0]]
                                                if c == 0 else []))
                lg = mk.sb([128, E], F32)
                r1 = v.run("tensor_tensor", lg[:], ps_r[:, 0:E], bias_bc[:],
                           ALU.add, deps=[mm, tb2])
                rps_free = r1
                yps_free[0] = r1
                pb = mk.sb([128, E], F32)
                r2 = s.run("activation", pb[:], lg[:], AF.Sigmoid, deps=[r1])
                v1 = mk.sb([128, 1], F32)
                v2 = mk.sb([128, 1], F32)
                m1 = mk.sb([128, E], F32)
                m2 = mk.sb([128, E], F32)
                pm = mk.sb([128, E], F32)
                r3 = v.run("reduce_max", v1[:], pb[:], axis=AXX, deps=[r2])
                r4 = v.run("tensor_scalar", m1[:], pb[:], v1[:, 0:1], None,
                           ALU.is_equal, deps=[r3])
                r5 = v.run("scalar_tensor_tensor", pm[:], m1[:], -1e9, pb[:],
                           ALU.mult, ALU.add, deps=[r4])
                r6 = v.run("reduce_max", v2[:], pm[:], axis=AXX, deps=[r5])
                r7 = v.run("tensor_scalar", m2[:], pm[:], v2[:, 0:1], None,
                           ALU.is_equal, deps=[r6])
                s12 = mk.sb([128, 1], F32)
                r8 = v.run("tensor_tensor", s12[:], v1[:], v2[:], ALU.add,
                           deps=[r3, r6])
                r9 = v.run("reciprocal", s12[:], s12[:], deps=[r8])
                r10 = v.run("tensor_scalar", m1[:], m1[:], v1[:, 0:1], None,
                            ALU.mult, deps=[r4, r5, w_free[j]])
                r11 = v.run("scalar_tensor_tensor", w_rt[j][:], m2[:],
                            v2[:, 0:1], m1[:], ALU.mult, ALU.add,
                            deps=[r7, r10])
                r12 = v.run("tensor_scalar", w_rt[j][:], w_rt[j][:],
                            s12[:, 0:1], None, ALU.mult, deps=[r9, r11])
                w_free[j] = r12
            xLT_free = None
            for ei, (e, Wg, Wu, Wd) in enumerate(wsrc):
                wb = 0
                wgb = ei % 2
                dwd_prev = wd_free[wb]
                io.dma(wd[wb][:], mk.wd_bf[ei],
                       deps=[dwd_prev] + list(getattr(mk, "cvt_done", [])))
                dwd = io.gtok()
                htoks = []
                gp = 0
                for hc in range(HC):
                    io.dma(wgu[wgb][:], mk.wgu_bf[ei, hc],
                           deps=[wgu_free[wgb]] +
                                list(getattr(mk, "cvt_done", [])))
                    dwgu = io.gtok()
                    for tcb in range(NC5):
                        gp ^= 1
                        mmg = mmu = None
                        for c in range(DC):
                            dd = [dwgu, gps_free[gp]] if c == 0 else []
                            mmg = t.run(
                                "matmul", ps_g[gp][:, 0:TW],
                                wgu[wgb][:, c * 128:(c + 1) * 128],
                                xLT[:, c * TB + tcb * TW:
                                    c * TB + (tcb + 1) * TW],
                                start=(c == 0), stop=(c == DC - 1),
                                deps=dd + [ln_toks[min(NTT - 1, (tcb + 1) * tpc - 1)][c]])
                        for c in range(DC):
                            dd = [dwgu, ups_free[gp]] if c == 0 else []
                            mmu = t.run(
                                "matmul", ps_u[gp][:, 0:TW],
                                wgu[wgb][:, D + c * 128: D + (c + 1) * 128],
                                xLT[:, c * TB + tcb * TW:
                                    c * TB + (tcb + 1) * TW],
                                start=(c == 0), stop=(c == DC - 1), deps=dd)
                        sg = sg2[gp]
                        a1 = s.run("activation", sg[:, 0:TW], ps_g[gp][:, 0:TW],
                                   AF.Silu, deps=[mmg, sg_free[gp]])
                        gps_free[gp] = a1
                        a2 = v.run("tensor_tensor",
                                   h_e[:, hc * TB + tcb * TW:
                                       hc * TB + (tcb + 1) * TW],
                                   ps_u[gp][:, 0:TW], sg[:, 0:TW], ALU.mult,
                                   deps=[mmu, a1, h_free])
                        ups_free[gp] = a2
                        sg_free[gp] = a2
                        htoks.append(a2)
                    wgu_free[wgb] = htoks[-1]
                acc_toks = []
                for j in range(NTT):
                    for dc in range(ND4):
                        di = dc % 2
                        mm_d = None
                        for hc in range(HC):
                            dd = []
                            if hc == 0:
                                dd = [dwd, yps_free[di]]
                            mm_d = t.run(
                                "matmul", ps_y[di][:, 0:NW],
                                h_e[:, hc * TB + j * 128:
                                    hc * TB + (j + 1) * 128],
                                wd[wb][:, hc * D + dc * NW:
                                       hc * D + (dc + 1) * NW],
                                start=(hc == 0), stop=(hc == HC - 1),
                                deps=dd + [htoks[hc * NC5 + j // tpc]])
                        if e is None:
                            ac = v.run("tensor_tensor",
                                       x2acc[j][:, dc * NW:(dc + 1) * NW],
                                       ps_y[di][:, 0:NW],
                                       x2acc[j][:, dc * NW:(dc + 1) * NW],
                                       ALU.add, deps=[mm_d, w_free[j]])
                        else:
                            ac = v.run("scalar_tensor_tensor",
                                       x2acc[j][:, dc * NW:(dc + 1) * NW],
                                       ps_y[di][:, 0:NW],
                                       w_rt[j][:, e:e + 1],
                                       x2acc[j][:, dc * NW:(dc + 1) * NW],
                                       ALU.mult, ALU.add,
                                       deps=[mm_d, w_free[j]])
                        acc_toks.append(ac)
                        yps_free[di] = ac
                h_free = acc_toks[-1]
                wd_free[wb] = acc_toks[-1]
            for j in range(NTT):
                iog.dma(dst_dram[i * TB + j * 128: i * TB + (j + 1) * 128, :],
                        x2acc[j][:], deps=[acc_toks[j * ND4 + dc]
                                           for dc in range(ND4)] + [acc_toks[-1]])
                x2_free[j] = iog.gtok()
                if final:
                    mk._final_tokens.append(x2_free[j])
            xLT_free = x2_free[0]





F32 = mybir.dt.float32
I32 = mybir.dt.int32
BF16 = mybir.dt.bfloat16
AF = mybir.ActivationFunctionType
ALU = mybir.AluOpType
AXX = mybir.AxisListType.X
TWO_PI = float(2.0 * math.pi)
HALF_PI = float(0.5 * math.pi)


def build_rglru(mk, x2_s, x3_s, h_s, arp_d, aip_d, Win, bin_d, Wgx, bgx_d,
                Wga, bga_d, Wout, bout_d):
    cfg = mk.cfg
    pr = mk.pr
    v, s, t, io, iog, ios = pr.v, pr.s, pr.t, pr.io, pr.iog, pr.ios
    S, D, NB, DH = cfg["S"], cfg["D"], cfg["NB"], cfg["DH"]
    DC = D // 128
    GC = DH // 128
    CPB = (D // NB) // 128
    assert DH // NB == 128
    TB = min(256, S)
    NBK = S // TB
    NTT = TB // 128

    with mk.scope():
        arp_sb, darp = mk.load_vec(arp_d, GC)
        aip_sb, daip = mk.load_vec(aip_d, GC)
        bin_sb, dbin = mk.load_vec(bin_d, DC)
        bgx_sb, dbgx = mk.load_vec(bgx_d, GC)
        bga_sb, dbga = mk.load_vec(bga_d, GC)
        m8sp = mk.sb([128, GC], F32)
        t_abs = s.run("activation", m8sp[:], arp_sb[:], AF.Abs, deps=[darp])
        e1 = mk.sb([128, GC], F32)
        cm1 = mk.const(-1.0)
        t_e = s.run("activation", e1[:], m8sp[:], AF.Exp, scale=cm1,
                    deps=[t_abs, mk.ctok(-1.0)])
        t_l = v.run("tensor_scalar", e1[:], e1[:], 1.0, None, ALU.add,
                    deps=[t_e])
        t_ln = s.run("activation", e1[:], e1[:], AF.Ln, deps=[t_l])
        t_mx = v.run("tensor_scalar", m8sp[:], arp_sb[:], 0.0, None, ALU.max,
                     deps=[darp, t_abs])
        t_sp = v.run("tensor_tensor", m8sp[:], m8sp[:], e1[:], ALU.add,
                     deps=[t_mx, t_ln])
        t_m8 = v.run("tensor_scalar", m8sp[:], m8sp[:], -8.0, None, ALU.mult,
                     deps=[t_sp])

        win_sb = mk.sb([128, DC * D], BF16)
        iog.dma(win_sb[:].rearrange("p (c n) -> p c n", n=D),
                Win.rearrange("(c p) m -> p c m", p=128))
        dwin = iog.gtok()
        bpc = D // NB // 128
        wgx_sb = mk.sb([128, NB * bpc * 128], BF16)
        iog.dma(wgx_sb[:].rearrange("p (n c f) -> p n c f", f=128, c=bpc),
                Wgx.rearrange("n (c p) f -> p n c f", p=128))
        wga_sb = mk.sb([128, NB * bpc * 128], BF16)
        iog.dma(wga_sb[:].rearrange("p (n c f) -> p n c f", f=128, c=bpc),
                Wga.rearrange("n (c p) f -> p n c f", p=128))
        dwg = iog.gtok()

        xblk = [mk.sb([128, D], F32) for _ in range(NTT)]
        xln = mk.sb([128, D], BF16)
        xLT = mk.sb([128, DC * TB], BF16)
        xh = mk.sb([128, DC * TB], F32)
        xhb = mk.sb([128, DC * TB], BF16)
        _mkt = lambda dt=F32: [mk.sb([128, TB], dt) for _ in range(2)]
        ga2, la2, th2, mag2 = _mkt(), _mkt(), _mkt(), _mkt()
        TH2, uin2, rr2, rm2 = _mkt(), _mkt(I32), _mkt(), _mkt()
        gx2, nrm2, cs2, sn2 = _mkt(), _mkt(), _mkt(), _mkt()
        xr2_, xi2_, tA2, tB2, gr2, gi2 = (_mkt(), _mkt(), _mkt(), _mkt(),
                                          _mkt(), _mkt())
        hstage = mk.sb([128, DC * TB], BF16)
        ones_t = mk.sb([128, TB], F32)
        c_on = v.run("memset", ones_t[:], 1.0)
        thc = mk.sb([128, GC], F32)
        grc = mk.sb([128, GC], F32)
        gic = mk.sb([128, GC], F32)
        z1 = v.run("memset", thc[:], 0.0)
        z2 = v.run("memset", grc[:], 0.0)
        z3 = v.run("memset", gic[:], 0.0)
        carry_tok = [[z1, z2, z3] for _ in range(GC)]

        ps_tr = mk.ps([128, 2048], BF16)
        ps_h = [mk.ps([128, 512], F32) for _ in range(2)]
        ps_g = [mk.ps([128, 512], F32) for _ in range(4)]

        x_free = [None] * NTT
        xln_free = None
        xLT_free = None
        xh_free = None
        tr_free = [None, None]
        hps_free = [None, None]
        gps_free = [None, None, None, None]
        hst_free = None
        prevs = [dict(e9=None, e2=None, e4=None, s3=None, e10=None, q7=None,
                      b4=None, b5=None) for _ in range(2)]

        NC5 = TB // 512 if TB >= 512 else 1
        TW = min(512, TB)

        for i in range(NBK):
            ln_toks = []
            for j in range(NTT):
                io.dma(xblk[j][:],
                       x2_s[i * TB + j * 128: i * TB + (j + 1) * 128, :],
                       deps=[x_free[j]])
                dx = io.gtok()
                tln = ln_apply(mk, xblk[j], xln[:], D, 1e-5,
                               deps=[dx, xln_free])
                x_free[j] = tln
                evs = []
                for c in range(DC):
                    sl = c % 2
                    tt = t.run("transpose",
                               ps_tr[:, sl * 1024:sl * 1024 + 128],
                               xln[:, c * 128:(c + 1) * 128], mk.identb_sb[:],
                               deps=[tln, tr_free[sl],
                                     xLT_free if (j == 0 and c == 0) else None,
                                     mk.identb_tok])
                    ev = v.run("tensor_copy",
                               xLT[:, c * TB + j * 128:
                                   c * TB + (j + 1) * 128],
                               ps_tr[:, sl * 1024:sl * 1024 + 128], deps=[tt])
                    tr_free[sl] = ev
                    evs.append(ev)
                xln_free = evs[-1]
                ln_toks.append(evs)
            xh_toks = []
            hp = 0
            for hc in range(DC):
                for tcb in range(NC5):
                    hp ^= 1
                    mm = None
                    for c in range(DC):
                        dd = [dwin, hps_free[hp]] if c == 0 else []
                        mm = t.run("matmul", ps_h[hp][:, 0:TW],
                                   win_sb[:, c * D + hc * 128:
                                          c * D + (hc + 1) * 128],
                                   xLT[:, c * TB + tcb * TW:
                                       c * TB + (tcb + 1) * TW],
                                   start=(c == 0), stop=(c == DC - 1),
                                   deps=dd + [ln_toks[min(
                                       NTT - 1, (tcb + 1) * (TW // 128) - 1)][c]])
                    w1 = v.run("tensor_scalar",
                               xh[:, hc * TB + tcb * TW:
                                  hc * TB + (tcb + 1) * TW],
                               ps_h[hp][:, 0:TW], bin_sb[:, hc:hc + 1], None,
                               ALU.add, deps=[mm, dbin, xh_free])
                    w2 = v.run("tensor_copy",
                               xhb[:, hc * TB + tcb * TW:
                                   hc * TB + (tcb + 1) * TW],
                               xh[:, hc * TB + tcb * TW:
                                  hc * TB + (tcb + 1) * TW], deps=[w1])
                    hps_free[hp] = w2
                    xh_toks.append(w2)
            xh_free = xh_toks[-1]
            xLT_free = xh_toks[-1]
            hev_all = []
            for gp2 in range(GC // 2):
                pair = [2 * gp2, 2 * gp2 + 1]
                dep_xh = xh_toks[-1]
                T = []
                for idx, gc in enumerate(pair):
                    par = gc % 2
                    T.append(dict(
                        gc=gc, qq=(v if par == 0 else pr.g),
                        prev=prevs[par],
                        gx=gx2[par], ga=ga2[par], la=la2[par], th=th2[par],
                        mag=mag2[par], nrm=nrm2[par], TH=TH2[par],
                        uin=uin2[par], rr=rr2[par], rm=rm2[par],
                        cs=cs2[par], sn=sn2[par], xr=xr2_[par], xi=xi2_[par],
                        tA=tA2[par], tB=tB2[par], gr=gr2[par], gi=gi2[par],
                        px=2 * idx, pa=2 * idx + 1))
                for d in T:
                    gc = d["gc"]
                    mmx = None
                    for c in range(bpc):
                        xi_c = gc * bpc + c
                        dd0 = [dwg, gps_free[d["px"]]] if c == 0 else []
                        mmx = t.run("matmul", ps_g[d["px"]][:, 0:TB],
                                    wgx_sb[:, xi_c * 128:(xi_c + 1) * 128],
                                    xhb[:, xi_c * TB:(xi_c + 1) * TB],
                                    start=(c == 0), stop=(c == bpc - 1),
                                    deps=dd0 + [dep_xh])
                    d["mmx"] = mmx
                    mma = None
                    for c in range(bpc):
                        xi_c = gc * bpc + c
                        dd0 = [gps_free[d["pa"]]] if c == 0 else []
                        mma = t.run("matmul", ps_g[d["pa"]][:, 0:TB],
                                    wga_sb[:, xi_c * 128:(xi_c + 1) * 128],
                                    xhb[:, xi_c * TB:(xi_c + 1) * TB],
                                    start=(c == 0), stop=(c == bpc - 1),
                                    deps=dd0)
                    d["mma"] = mma
                for d in T:
                    gc = d["gc"]
                    a1 = s.run("activation", d["gx"][:],
                               ps_g[d["px"]][:, 0:TB], AF.Sigmoid,
                               bias=bgx_sb[:, gc:gc + 1],
                               deps=[d["mmx"], dbgx, d["prev"]["e9"]])
                    gps_free[d["px"]] = a1
                    a2 = s.run("activation", d["ga"][:],
                               ps_g[d["pa"]][:, 0:TB], AF.Sigmoid,
                               bias=bga_sb[:, gc:gc + 1],
                               deps=[d["mma"], dbga, d["prev"]["e2"]])
                    gps_free[d["pa"]] = a2
                    d["a1"], d["a2"] = a1, a2
                for d in T:
                    gc = d["gc"]
                    d["e1"] = v.run("tensor_scalar", d["la"][:], d["ga"][:],
                                    m8sp[:, gc:gc + 1], None, ALU.mult,
                                    deps=[d["a2"], t_m8, d["prev"]["e4"]])
                    d["e2t"] = v.run("tensor_scalar", d["th"][:], d["ga"][:],
                                     aip_sb[:, gc:gc + 1], None, ALU.mult,
                                     deps=[d["a2"], daip])
                for d in T:
                    d["e3"] = s.run("activation", d["mag"][:], d["la"][:],
                                    AF.Exp, deps=[d["e1"], d["prev"]["s3"]])
                for d in T:
                    d["e4"] = v.run("tensor_tensor", d["nrm"][:], d["mag"][:],
                                    d["mag"][:], ALU.mult,
                                    deps=[d["e3"], d["prev"]["e10"]])
                    d["e5"] = v.run("tensor_scalar", d["nrm"][:], d["nrm"][:],
                                    -1.0, 1.0, ALU.mult, ALU.add,
                                    deps=[d["e4"]])
                for d in T:
                    d["e6a"] = s.run("activation", d["nrm"][:], d["nrm"][:],
                                     AF.Ln, deps=[d["e5"]])
                for d in T:
                    d["e6"] = s.run("activation", d["nrm"][:], d["nrm"][:],
                                    AF.Exp, scale=mk.const(0.5),
                                    deps=[d["e6a"], mk.ctok(0.5)])
                for d in T:
                    gc = d["gc"]
                    qq = d["qq"]
                    d["e7"] = qq.run("tensor_tensor", d["xr"][:], d["gx"][:],
                                     xh[:, gc * TB:(gc + 1) * TB], ALU.mult,
                                     deps=[d["a1"], dep_xh])
                    d["e8"] = qq.run("tensor_tensor", d["xr"][:], d["xr"][:],
                                     d["nrm"][:], ALU.mult,
                                     deps=[d["e7"], d["e6"]])
                    d["e9"] = qq.run("tensor_tensor", d["xi"][:], d["gx"][:],
                                     xh[:, (GC + gc) * TB:(GC + gc + 1) * TB],
                                     ALU.mult, deps=[d["a1"], dep_xh])
                    d["e10"] = qq.run("tensor_tensor", d["xi"][:], d["xi"][:],
                                      d["nrm"][:], ALU.mult,
                                      deps=[d["e9"], d["e6"]])
                for d in T:
                    gc = d["gc"]
                    s1 = v.run("tensor_tensor_scan", d["TH"][:], ones_t[:],
                               d["th"][:], thc[:, gc:gc + 1], ALU.mult,
                               ALU.add, deps=[d["e2t"], c_on,
                                              carry_tok[gc][0]])
                    s1b = v.run("tensor_copy", thc[:, gc:gc + 1],
                                d["TH"][:, TB - 1:TB], deps=[s1])
                    carry_tok[gc][0] = s1b
                    d["s1"] = s1
                for d in T:
                    d["q1"] = v.run("tensor_scalar", d["uin"][:], d["TH"][:],
                                    1.0 / TWO_PI, None, ALU.mult,
                                    deps=[d["s1"]])
                    d["q2"] = v.run("scalar_tensor_tensor", d["rr"][:],
                                    d["uin"][:], -TWO_PI, d["TH"][:],
                                    ALU.mult, ALU.add,
                                    deps=[d["q1"], d["prev"]["q7"]])
                    d["q2a"] = v.run("tensor_scalar", d["rm"][:], d["rr"][:],
                                     math.pi, None, ALU.is_ge,
                                     deps=[d["q2"]])
                    d["q2b"] = v.run("scalar_tensor_tensor", d["rr"][:],
                                     d["rm"][:], -TWO_PI, d["rr"][:],
                                     ALU.mult, ALU.add, deps=[d["q2a"]])
                for d in T:
                    d["q3"] = s.run("activation", d["sn"][:], d["rr"][:],
                                    AF.Sin, deps=[d["q2b"], d["prev"]["b4"]])
                for d in T:
                    d["q4"] = v.run("tensor_scalar", d["rr"][:], d["TH"][:],
                                    HALF_PI, None, ALU.add,
                                    deps=[d["q2b"], d["q3"]])
                    d["q5"] = v.run("tensor_scalar", d["uin"][:], d["rr"][:],
                                    1.0 / TWO_PI, None, ALU.mult,
                                    deps=[d["q4"], d["q2"]])
                    d["q6"] = v.run("scalar_tensor_tensor", d["rr"][:],
                                    d["uin"][:], -TWO_PI, d["rr"][:],
                                    ALU.mult, ALU.add, deps=[d["q5"]])
                    d["q6a"] = v.run("tensor_scalar", d["rm"][:], d["rr"][:],
                                     math.pi, None, ALU.is_ge,
                                     deps=[d["q6"]])
                    d["q6b"] = v.run("scalar_tensor_tensor", d["rr"][:],
                                     d["rm"][:], -TWO_PI, d["rr"][:],
                                     ALU.mult, ALU.add, deps=[d["q6a"]])
                for d in T:
                    d["q7"] = s.run("activation", d["cs"][:], d["rr"][:],
                                    AF.Sin, deps=[d["q6b"], d["prev"]["b5"]])
                for d in T:
                    qq = d["qq"]
                    d["r1"] = qq.run("tensor_tensor", d["tA"][:], d["cs"][:],
                                     d["xr"][:], ALU.mult,
                                     deps=[d["q7"], d["e8"]])
                    d["r2"] = qq.run("tensor_tensor", d["tB"][:], d["sn"][:],
                                     d["xi"][:], ALU.mult,
                                     deps=[d["q3"], d["e10"]])
                    d["r3"] = qq.run("tensor_tensor", d["tA"][:], d["tA"][:],
                                     d["tB"][:], ALU.add,
                                     deps=[d["r1"], d["r2"]])
                    d["r4"] = qq.run("tensor_tensor", d["tB"][:], d["cs"][:],
                                     d["xi"][:], ALU.mult,
                                     deps=[d["q7"], d["e10"], d["r3"]])
                    d["r5"] = qq.run("tensor_tensor", d["xr"][:], d["sn"][:],
                                     d["xr"][:], ALU.mult,
                                     deps=[d["q3"], d["r3"]])
                    d["r6"] = qq.run("tensor_tensor", d["tB"][:], d["tB"][:],
                                     d["xr"][:], ALU.subtract,
                                     deps=[d["r4"], d["r5"]])
                for d in T:
                    gc = d["gc"]
                    s2 = v.run("tensor_tensor_scan", d["gr"][:], d["mag"][:],
                               d["tA"][:], grc[:, gc:gc + 1], ALU.mult,
                               ALU.add, deps=[d["e3"], d["r3"],
                                              carry_tok[gc][1]])
                    s2b = v.run("tensor_copy", grc[:, gc:gc + 1],
                                d["gr"][:, TB - 1:TB], deps=[s2])
                    carry_tok[gc][1] = s2b
                    s3 = v.run("tensor_tensor_scan", d["gi"][:], d["mag"][:],
                               d["tB"][:], gic[:, gc:gc + 1], ALU.mult,
                               ALU.add, deps=[d["e3"], d["r6"],
                                              carry_tok[gc][2]])
                    s3b = v.run("tensor_copy", gic[:, gc:gc + 1],
                                d["gi"][:, TB - 1:TB], deps=[s3])
                    carry_tok[gc][2] = s3b
                    d["s2"], d["s3"] = s2, s3
                for d in T:
                    gc = d["gc"]
                    qq = d["qq"]
                    b1 = qq.run("tensor_tensor", d["tA"][:], d["cs"][:],
                                d["gr"][:], ALU.mult,
                                deps=[d["s2"], d["q7"], d["r6"]])
                    b2 = qq.run("tensor_tensor", d["tB"][:], d["sn"][:],
                                d["gi"][:], ALU.mult,
                                deps=[d["s3"], d["q3"], d["r6"]])
                    b3 = qq.run("tensor_tensor",
                                hstage[:, gc * TB:(gc + 1) * TB],
                                d["tA"][:], d["tB"][:], ALU.subtract,
                                deps=[b1, b2, hst_free])
                    b4 = qq.run("tensor_tensor", d["tA"][:], d["sn"][:],
                                d["gr"][:], ALU.mult,
                                deps=[d["s2"], d["q3"], b3])
                    b5 = qq.run("tensor_tensor", d["tB"][:], d["cs"][:],
                                d["gi"][:], ALU.mult,
                                deps=[d["s3"], d["q7"], b3])
                    b6 = qq.run("tensor_tensor",
                                hstage[:, (GC + gc) * TB:(GC + gc + 1) * TB],
                                d["tA"][:], d["tB"][:], ALU.add,
                                deps=[b4, b5, hst_free])
                    d["prev"].update(e9=d["e9"], e2=d["e2t"], e4=d["e4"],
                                     s3=d["s3"], e10=d["e10"], q7=d["q7"],
                                     b4=b4, b5=b5)
                    hev_all.append(b3)
                    hev_all.append(b6)
            for r in range(NTT):
                ios.dma(h_s[i * NTT + r].rearrange("p (c f) -> p c f", f=128),
                        hstage[:].rearrange("p (c f) -> p c f", f=TB)
                        [:, :, r * 128:(r + 1) * 128],
                        deps=hev_all if r == 0 else [])
            hst_free = ios.gtok()

    wout_half_pass(mk, x2_s, x3_s, h_s, Wout, bout_d)


def wout_half_pass(mk, x2_s, x3_s, h_s, Wout, bout_d):
    cfg = mk.cfg
    pr = mk.pr
    v, t, io, iog, ios = pr.v, pr.t, pr.io, pr.iog, pr.ios
    S, D = cfg["S"], cfg["D"]
    SH = S // 2
    DC = D // 128
    ND4 = max(1, D // 512)
    NW = min(512, D)
    NT2 = SH // 128
    with mk.scope():
        wout_sb = mk.sb([128, DC * D], BF16)
        iog.dma(wout_sb[:].rearrange("p (c n) -> p c n", n=D),
                Wout.rearrange("(c p) m -> p c m", p=128))
        dwo = iog.gtok()
        boutrow = mk.sb([1, D], F32)
        io.dma(boutrow[:], bout_d)
        dbo = io.gtok()
        onescol = mk.sb([1, 128], F32)
        c1 = v.run("memset", onescol[:], 1.0)
        bout_bc = mk.sb([128, D], F32)
        ps_b = mk.ps([128, 512], F32)
        btoks = []
        for n in range(ND4):
            bm = t.run("matmul", ps_b[:, 0:NW], onescol[0:1, :],
                       boutrow[:, n * NW:(n + 1) * NW], start=True, stop=True,
                       deps=[dbo, c1] + ([btoks[-1]] if btoks else []))
            be = v.run("tensor_copy", bout_bc[:, n * NW:(n + 1) * NW],
                       ps_b[:, 0:NW], deps=[bm])
            btoks.append(be)
        hA = mk.sb([128, DC * 128], BF16)
        hB = mk.sb([128, DC * 128], BF16)
        hO = mk.sb([128, DC * 128], BF16)
        x2A = mk.sb([128, D], F32)
        x2B = mk.sb([128, D], F32)
        x3t = mk.sb([128, D], F32)
        ps_y = [mk.ps([128, 512], F32) for _ in range(2)]
        hload_free = None
        x3_free = None
        yps_free = [None, None]
        for i in range(NT2):
            io.dma(hA[:], h_s[i], deps=[hload_free])
            io.dma(hB[:], h_s[SH // 128 + i], deps=[hload_free])
            io.dma(x2A[:], x2_s[i * 128:(i + 1) * 128, :], deps=[x3_free])
            io.dma(x2B[:], x2_s[SH + i * 128: SH + (i + 1) * 128, :],
                   deps=[x3_free])
            dl = io.gtok()
            s1 = v.run("tensor_scalar", hO[:], hA[:],
                       mk.halfsel_sb[:, 0:1], None, ALU.mult,
                       deps=[dl, mk.halfsel_tok, hload_free])
            s2 = v.run("scalar_tensor_tensor", hO[:], hB[:],
                       mk.halfsel_sb[:, 1:2], hO[:], ALU.mult, ALU.add,
                       deps=[s1])
            s3 = v.run("tensor_scalar", x2A[:], x2A[:],
                       mk.halfsel_sb[:, 0:1], None, ALU.mult, deps=[dl])
            s4 = v.run("scalar_tensor_tensor", x2A[:], x2B[:],
                       mk.halfsel_sb[:, 1:2], x2A[:], ALU.mult, ALU.add,
                       deps=[s3])
            hload_free = s2
            ytoks = []
            for n in range(ND4):
                np_ = n % 2
                mm = None
                for c in range(DC):
                    dd = [dwo, s2, yps_free[np_]] if c == 0 else []
                    mm = t.run("matmul", ps_y[np_][:, 0:NW],
                               hO[:, c * 128:(c + 1) * 128],
                               wout_sb[:, c * D + n * NW:
                                       c * D + (n + 1) * NW],
                               start=(c == 0), stop=(c == DC - 1), deps=dd)
                y1 = v.run("scalar_tensor_tensor",
                           x3t[:, n * NW:(n + 1) * NW], ps_y[np_][:, 0:NW],
                           1.0, bout_bc[:, n * NW:(n + 1) * NW], ALU.mult,
                           ALU.add, deps=[mm, btoks[n], x3_free])
                y2 = v.run("tensor_tensor", x3t[:, n * NW:(n + 1) * NW],
                           x3t[:, n * NW:(n + 1) * NW],
                           x2A[:, n * NW:(n + 1) * NW], ALU.add,
                           deps=[y1, s4])
                yps_free[np_] = y2
                ytoks.append(y2)
            ios.dma(x3_s[i * 128:(i + 1) * 128, :], x3t[:], deps=ytoks)
            x3_free = ios.gtok()








F32 = mybir.dt.float32
I32 = mybir.dt.int32
BF16 = mybir.dt.bfloat16
AF = mybir.ActivationFunctionType
ALU = mybir.AluOpType
AXX = mybir.AxisListType.X
TWO_PI = float(2.0 * math.pi)


def make_cfg(S=4096, D=2048, NH=16, HD=128, W=1024, HF=1408, E=8, NB=8,
             theta=10000.0):
    return dict(S=S, D=D, NH=NH, HD=HD, W=W, HF=HF, E=E, NB=NB, DH=D // 2,
                theta=theta)


class MK:
    def __init__(self, cfg, debug_outputs=()):
        self.cfg = cfg
        self.debug_outputs = set(debug_outputs)
        self.nc = bass.Bass("TRN2", target_bir_lowering=False, debug=False)
        self.es = ExitStack()
        self.es_local = self.es
        self.pr = Prog(self.nc, self.es)
        self.dram = {}
        self._n = 0
        self._final_tokens = []

    @contextmanager
    def scope(self):
        old = self.es_local
        self.es_local = ExitStack()
        try:
            yield
        finally:
            self.es_local.close()
            self.es_local = old
            self.pr.barrier()

    def sb(self, shape, dt=F32, name=None):
        self._n += 1
        nm = f"{name}_{self._n}" if name else f"sb{self._n}"
        return self.es_local.enter_context(
            self.nc.sbuf_tensor(nm, shape, dt))

    def ps(self, shape, dt=F32, name=None):
        self._n += 1
        bank = 512 if dt == F32 or dt == I32 else 1024
        shape = list(shape)
        shape[-1] = ((shape[-1] + bank - 1) // bank) * bank
        return self.es_local.enter_context(
            self.nc.psum_tensor(name or f"ps{self._n}", shape, dt))

    def din(self, name, shape, dt=F32):
        ap = self.nc.dram_tensor(name, shape, dt, kind="ExternalInput").ap()
        self.dram[name] = ap
        return ap

    def dout(self, name, shape, dt=F32):
        ap = self.nc.dram_tensor(name, shape, dt, kind="ExternalOutput").ap()
        self.dram[name] = ap
        return ap

    def dscratch(self, name, shape, dt=F32):
        kind = "ExternalOutput" if name in self.debug_outputs else "Internal"
        ap = self.nc.dram_tensor(name, shape, dt, kind=kind).ap()
        self.dram[name] = ap
        return ap

    def setup_consts(self):
        self._cvals = {}
        self._ctile = self.sb([128, 16], F32, "cpool")
        self._cn = 0

    def const(self, val):
        val = float(val)
        if val not in self._cvals:
            i = self._cn
            assert i < 16
            self._cn += 1
            tok = self.pr.v.run("memset", self._ctile[:, i:i + 1], val)
            self._cvals[val] = (self._ctile[:, i:i + 1], tok)
        return self._cvals[val][0]

    def ctok(self, val):
        return self._cvals[float(val)][1]

    def load_vec(self, dram_ap, C, dt=F32):
        t = self.sb([128, C], dt)
        self.pr.io.dma(t[:], dram_ap)
        return t, self.pr.io.gtok()


def build(cfg, debug_outputs=(), stop=None):

    mk = MK(cfg, debug_outputs)
    nc, pr = mk.nc, mk.pr
    S, D, E, HF, HD = cfg["S"], cfg["D"], cfg["E"], cfg["HF"], cfg["HD"]
    SH = S // 2

    x_in = mk.din("x", [S, D], F32)
    Wqkv = mk.din("L0_attn_a_Wqkv", [D, 3 * D], F32)
    qnw_d = mk.din("qnw_t", [128, D // 128], F32)
    knw_d = mk.din("knw_t", [128, D // 128], F32)
    Wo = mk.din("L0_attn_a_Wo", [D, D], F32)
    arp_d = mk.din("arp_t", [128, cfg["DH"] // 128], F32)
    aip_d = mk.din("aip_t", [128, cfg["DH"] // 128], F32)
    Win = mk.din("L0_attn_b_W_in", [D, D], F32)
    bin_d = mk.din("bin_t", [128, D // 128], F32)
    Wgx = mk.din("L0_attn_b_Wg_x", [cfg["NB"], D // cfg["NB"],
                                    cfg["DH"] // cfg["NB"]], F32)
    bgx_d = mk.din("bgx_t", [128, cfg["DH"] // 128], F32)
    Wga = mk.din("L0_attn_b_Wg_a", [cfg["NB"], D // cfg["NB"],
                                    cfg["DH"] // cfg["NB"]], F32)
    bga_d = mk.din("bga_t", [128, cfg["DH"] // 128], F32)
    Wout = mk.din("L0_attn_b_W_out", [D, D], F32)
    bout_d = mk.din("bout_row", [1, D], F32)
    Wg_s = mk.din("L0_ffn_Wg_s", [D, HF], F32)
    Wu_s = mk.din("L0_ffn_Wu_s", [D, HF], F32)
    Wd_s = mk.din("L0_ffn_Wd_s", [HF, D], F32)
    Wg_e = mk.din("L0_ffn_Wg", [E, D, HF], F32)
    Wu_e = mk.din("L0_ffn_Wu", [E, D, HF], F32)
    Wd_e = mk.din("L0_ffn_Wd", [E, HF, D], F32)
    Wr = mk.din("L0_ffn_Wr", [D, E], F32)
    brrb_d = mk.din("brrb_row", [1, E], F32)
    br_d = mk.din("br_row", [1, E], F32)
    rb_d = mk.din("rb_row", [1, E], F32)
    identb = mk.din("identb", [128, 128], BF16)
    m0add = mk.din("m0add", [128, 128], BF16)
    m8add = mk.din("m8add", [128, 128], BF16)
    halfsel = mk.din("halfsel", [128, 2], F32)
    cos_d = mk.din("cos_t", [128, S], F32)
    sin_d = mk.din("sin_t", [128, S], F32)
    out = mk.dout("out", [SH, D], F32)

    NT = S // 128
    xT_s = mk.dscratch("xT_s", [NT, 128, D], BF16)
    q_s = mk.dscratch("q_s", [NT, 128, D], BF16)
    k_s = mk.dscratch("k_s", [NT, 128, D], BF16)
    v_s = mk.dscratch("v_s", [S, D], BF16)
    o_s = mk.dscratch("o_s", [NT, 128, D], BF16)
    x1_s = mk.dscratch("x1_s", [S, D], F32)
    x2_s = mk.dscratch("x2_s", [S, D], F32)
    h_s = mk.dscratch("h_s", [NT, 128, D], BF16)
    x3_s = mk.dscratch("x3_s", [SH, D], F32)
    HCW = HF // 128
    wgu_bf = mk.dscratch("wgu_bf", [E + 1, HCW, 128, 2 * D], BF16)
    wd_bf = mk.dscratch("wd_bf", [E + 1, 128, HCW * D], BF16)
    mk.wgu_bf, mk.wd_bf = wgu_bf, wd_bf
    mk.ffn_w = (Wg_s, Wu_s, Wd_s, Wg_e, Wu_e, Wd_e)

    mk.setup_consts()
    mk._scrap = mk.sb([128, max(D, HF)], F32, "scrap")
    mk._scrap_free = None
    mk.identb_sb = mk.sb([128, 128], BF16, "identb_sb")
    pr.io.dma(mk.identb_sb[:], identb)
    mk.m0_sb = mk.sb([128, 128], BF16, "m0_sb")
    pr.io.dma(mk.m0_sb[:], m0add)
    mk.m8_sb = mk.sb([128, 128], BF16, "m8_sb")
    pr.io.dma(mk.m8_sb[:], m8add)
    gl = pr.io.gtok()
    mk.identb_tok = mk.m0_tok = mk.m8_tok = gl
    mk.cos_d = cos_d
    mk.sin_d = sin_d
    mk.qnw_sb, mk.qnw_tok = mk.load_vec(qnw_d, D // 128)
    mk.knw_sb, mk.knw_tok = mk.load_vec(knw_d, D // 128)
    mk.halfsel_sb, mk.halfsel_tok = mk.load_vec(halfsel, 2)

    mk.stop = stop
    build_attn_a(mk, x_in, Wqkv, Wo, xT_s, q_s, k_s, v_s, o_s, x1_s)
    if stop is None or stop in ("ffn1", "rglru", "ffn2"):
        build_ffn(mk, x1_s, x2_s, S, Wg_s, Wu_s, Wd_s, Wg_e, Wu_e, Wd_e,
                  Wr, br_d, rb_d, final=False)
        if stop is None or stop in ("rglru", "ffn2"):
            build_rglru(mk, x2_s, x3_s, h_s, arp_d, aip_d, Win, bin_d, Wgx,
                        bgx_d, Wga, bga_d, Wout, bout_d)
            if stop is None or stop == "ffn2":
                build_ffn(mk, x3_s, out, SH, Wg_s, Wu_s, Wd_s, Wg_e, Wu_e,
                          Wd_e, Wr, br_d, rb_d, final=True)

    if not mk._final_tokens:
        toks = []
        for q in pr.queues():
            if len(q.sems) > 1:
                for sem, cnt in zip(q.sems, q.counts):
                    if cnt > 0:
                        toks.append((sem, cnt))
            else:
                t_ = q.tok()
                if t_ is not None:
                    toks.append(t_)
        mk._final_tokens = toks
    with nc.Block() as block:
        pr.emit(block, final_waits=mk._final_tokens)
    mk.es.close()
    return mk


def host_consts(cfg):
    S, HD, W, theta = cfg["S"], cfg["HD"], cfg["W"], cfg["theta"]
    import ml_dtypes
    d2 = HD // 2
    inv = (1.0 / theta ** (np.arange(0, HD, 2) / HD)).astype(np.float64)
    pos = np.arange(S, dtype=np.float64)
    fr = pos[None, :] * inv[:, None]
    cos_t = np.cos(fr).astype(np.float32)
    sin_t = np.sin(fr).astype(np.float32)
    reps = 128 // d2
    cos_t = np.concatenate([cos_t] * reps, axis=0)
    sin_t = np.concatenate([sin_t] * reps, axis=0)
    ident = np.eye(128, dtype=ml_dtypes.bfloat16)
    ii = np.arange(128)
    m0 = np.where(ii[None, :] >= ii[:, None] + 1, 0.0, -1e7)
    m8 = np.where(ii[None, :] <= ii[:, None], 0.0, -1e7)
    return dict(identb=ident,
                m0add=m0.astype(ml_dtypes.bfloat16),
                m8add=m8.astype(ml_dtypes.bfloat16),
                cos_t=cos_t, sin_t=sin_t)


def tile_vec(vec, C=None):
    v = np.asarray(vec).reshape(-1)
    n = v.shape[0] // 128
    return np.ascontiguousarray(v.reshape(n, 128).T).astype(np.float32)


def host_inputs(cfg, core, full_inputs):
    fi = full_inputs
    b, half = core // 2, core % 2
    if "L0_attn_a_hidden_states" in fi:
        d = {"x": np.ascontiguousarray(np.asarray(
            fi["L0_attn_a_hidden_states"])[b])}
    else:
        d = {"x": np.asarray(fi["x"])}
    for k in ["L0_attn_a_Wqkv", "L0_attn_a_Wo", "L0_attn_b_W_in",
              "L0_attn_b_Wg_x", "L0_attn_b_Wg_a", "L0_attn_b_W_out",
              "L0_ffn_Wg_s", "L0_ffn_Wu_s", "L0_ffn_Wd_s", "L0_ffn_Wg",
              "L0_ffn_Wu", "L0_ffn_Wd", "L0_ffn_Wr"]:
        d[k] = fi[k]
    d["qnw_t"] = tile_vec(fi["L0_attn_a_q_norm_w"])
    d["knw_t"] = tile_vec(fi["L0_attn_a_k_norm_w"])
    d["arp_t"] = tile_vec(fi["L0_attn_b_a_real_param"])
    d["aip_t"] = tile_vec(fi["L0_attn_b_a_imag_param"])
    d["bin_t"] = tile_vec(fi["L0_attn_b_b_in"])
    d["bgx_t"] = tile_vec(fi["L0_attn_b_bg_x"])
    d["bga_t"] = tile_vec(fi["L0_attn_b_bg_a"])
    d["bout_row"] = np.ascontiguousarray(
        fi["L0_attn_b_b_out"].reshape(1, -1)).astype(np.float32)
    d["br_row"] = np.ascontiguousarray(
        fi["L0_ffn_br"].reshape(1, -1)).astype(np.float32)
    d["rb_row"] = np.ascontiguousarray(
        fi["L0_ffn_rb"].reshape(1, -1)).astype(np.float32)
    d["brrb_row"] = d["br_row"]
    hs = np.zeros((128, 2), np.float32)
    hs[:, half] = 1.0
    d["halfsel"] = hs
    d.update(host_consts(cfg))
    return d


_NCORES = 8


def kernel(**inputs):
    cfg = make_cfg()
    mk = build(cfg)
    names = set(mk.dram.keys())
    in_maps = []
    for c in range(_NCORES):
        im = host_inputs(cfg, c, inputs)
        in_maps.append({k: v for k, v in im.items() if k in names})
    res = run_bass_kernel_spmd(mk.nc, in_maps, core_ids=list(range(_NCORES)))
    S, D = cfg["S"], cfg["D"]
    out = np.empty((_NCORES // 2, S, D), np.float32)
    for c in range(_NCORES):
        b, half = c // 2, c % 2
        out[b, half * (S // 2):(half + 1) * (S // 2), :] = \
            np.asarray(res.results[c]["out"])
    return out


# version 16
# speedup vs baseline: 1.2194x; 1.0234x over previous
import math
from contextlib import ExitStack, contextmanager
import numpy as np
import concourse.bass as bass
import concourse.mybir as mybir
from concourse.bass_utils import run_bass_kernel_spmd



_GSEQ = [0]


class Tok:
    __slots__ = ("q", "i", "used", "val")

    def __init__(self, q, i):
        self.q = q
        self.i = i
        self.used = False
        self.val = None


class _Op:
    __slots__ = ("seq", "waits", "fn", "sem", "inc", "tok")

    def __init__(self, seq, waits, fn, sem, inc, tok):
        self.seq = seq
        self.waits = waits
        self.fn = fn
        self.sem = sem
        self.inc = inc
        self.tok = tok


class _Q:
    def __init__(self, name, sems):
        self.name = name
        self.sems = sems
        self.sem = sems[0]
        self.ops = []
        self.cur = 0
        self.counts = [0] * len(sems)
        self.group_open = False
        self._barrier_deps = None

    def _mark(self, deps):
        out = []
        if self._barrier_deps:
            bd, self._barrier_deps = self._barrier_deps, None
            for d in bd:
                out.extend(self._mark([d]))
        for d in deps:
            if d is None:
                continue
            if isinstance(d, Tok):
                d.used = True
                out.append(d)
            else:
                sem, val = d
                if val > 0:
                    out.append(d)
        return out

    def run(self, method, *args, deps=(), **kwargs):
        waits = self._mark(deps)

        def fn(e):
            return getattr(e, method)(*args, **kwargs)
        tok = Tok(self, len(self.ops))
        _GSEQ[0] += 1
        self.ops.append(_Op(_GSEQ[0], waits, fn, None, 1, tok))
        return tok

    def raw(self, fn, deps=()):
        waits = self._mark(deps)
        tok = Tok(self, len(self.ops))
        _GSEQ[0] += 1
        self.ops.append(_Op(_GSEQ[0], waits, fn, None, 1, tok))
        return tok

    def tok(self):
        for op in reversed(self.ops):
            if op.tok is not None:
                op.tok.used = True
                return op.tok
        return None

    def dma(self, out, in_, deps=(), **kwargs):
        if not self.group_open:
            sem = self.sems[self.cur]
            if self.counts[self.cur] > 0:
                self._barrier_deps = (self._barrier_deps or []) + \
                    [(sem, self.counts[self.cur])]
            self.group_open = True
        waits = self._mark(deps)

        def fn(e):
            return e.dma_start(out=out, in_=in_, **kwargs)
        sem = self.sems[self.cur]
        self.counts[self.cur] += 16
        _GSEQ[0] += 1
        self.ops.append(_Op(_GSEQ[0], waits, fn, sem, 16, None))

    def gtok(self):
        assert self.group_open
        tok = (self.sems[self.cur], self.counts[self.cur])
        self.cur = (self.cur + 1) % len(self.sems)
        self.group_open = False
        return tok

    def finalize(self):
        cnt = 0
        for op in self.ops:
            if op.tok is not None:
                if op.tok.used:
                    cnt += 1
                    op.sem = self.sem
                    op.inc = 1
                    op.tok.val = cnt
                else:
                    op.sem = None


def _emit(e, *queues, final_waits=()):
    ops = sorted([op for q in queues for op in q.ops], key=lambda o: o.seq)
    waited = {}

    def do_wait(sem, val):
        key = id(sem)
        if waited.get(key, 0) >= val:
            return
        waited[key] = val
        e.wait_ge(sem, val)

    for op in ops:
        for w in op.waits:
            if isinstance(w, Tok):
                assert w.val is not None, "wait on un-finalized token"
                do_wait(w.q.sem, w.val)
            else:
                do_wait(w[0], w[1])
        ins = op.fn(e)
        if op.sem is not None:
            ins.then_inc(op.sem, op.inc)
    for w in final_waits:
        if isinstance(w, Tok):
            do_wait(w.q.sem, w.val)
        else:
            do_wait(w[0], w[1])


class Prog:
    def __init__(self, nc, es, dma_pool=10):
        self.nc = nc
        _GSEQ[0] = 0
        mk = lambda name: es.enter_context(nc.semaphore(name))
        self.v = _Q("v", [mk("q_v")])
        self.s = _Q("s", [mk("q_s")])
        self.t = _Q("t", [mk("q_t")])
        self.g = _Q("g", [mk("q_g")])
        self.io = _Q("io", [mk(f"q_io{i}") for i in range(dma_pool)])
        self.ios = _Q("ios", [mk(f"q_is{i}") for i in range(dma_pool)])
        self.iog = _Q("iog", [mk(f"q_ig{i}") for i in range(dma_pool)])
        self.cvt = _Q("cvt", [mk(f"q_cv{i}") for i in range(4)])

    def queues(self):
        return [self.v, self.s, self.t, self.g, self.io, self.ios, self.iog,
                self.cvt]

    def barrier_queues(self):
        return [q for q in self.queues() if q.name != "cvt"]

    def barrier(self):
        toks = []
        for q in self.barrier_queues():
            if len(q.sems) > 1:
                assert not q.group_open, "close dma group before barrier"
                for sem, cnt in zip(q.sems, q.counts):
                    if cnt > 0:
                        toks.append((sem, cnt))
            else:
                t = q.tok()
                if t is not None:
                    toks.append(t)
        for q in self.barrier_queues():
            bd = [t for t in toks
                  if not (isinstance(t, Tok) and t.q is q)]
            q._barrier_deps = (q._barrier_deps or []) + bd

    def emit(self, block, final_waits=()):
        for q in self.queues():
            q.finalize()

        @block.sync
        def _(e):
            _emit(e, self.io)

        @block.vector
        def _(e):
            _emit(e, self.v)

        @block.scalar
        def _(e):
            _emit(e, self.ios, self.s)

        @block.tensor
        def _(e):
            _emit(e, self.t)

        @block.gpsimd
        def _(e):
            _emit(e, self.iog, self.g, self.cvt, final_waits=final_waits)




F32 = mybir.dt.float32
BF16 = mybir.dt.bfloat16
AF = mybir.ActivationFunctionType
ALU = mybir.AluOpType
AXX = mybir.AxisListType.X


def ln_apply(mk, x_tile_sb, out_ap, D, eps, deps):
    v, s = mk.pr.v, mk.pr.s
    if getattr(mk, "_ln_scope", None) is not mk.es_local:
        mk._ln_scope = mk.es_local
        mk._ln_tiles = [mk.sb([128, 1]) for _ in range(6)]
        mk._ln_prev = {}
    mu, ssq, var, rstd, mu2, sq = mk._ln_tiles
    lp = mk._ln_prev
    t1 = v.run("reduce_sum", mu[:], x_tile_sb[:], axis=AXX, deps=deps)
    t2 = s.run("activation", mk._scrap[:, 0:D], x_tile_sb[:], AF.Square,
               accum_out=ssq[:],
               deps=list(deps) + [mk._scrap_free, lp.get("t4")])
    mk._scrap_free = t2
    t3 = v.run("tensor_scalar", mu[:], mu[:], 1.0 / D, None, ALU.mult,
               deps=[t1])
    t4 = v.run("tensor_scalar", var[:], ssq[:], 1.0 / D, eps, ALU.mult,
               ALU.add, deps=[t2, lp.get("t7")])
    t5 = v.run("tensor_tensor", mu2[:], mu[:], mu[:], ALU.mult, deps=[t3])
    t6 = v.run("tensor_tensor", var[:], var[:], mu2[:], ALU.subtract,
               deps=[t4, t5])
    t7 = s.run("activation", sq[:], var[:], AF.Ln, deps=[t6])
    t8 = s.run("activation", rstd[:], sq[:], AF.Exp, scale=mk.const(-0.5),
               deps=[t7, mk.ctok(-0.5), lp.get("t9")])
    t9 = v.run("tensor_scalar", out_ap, x_tile_sb[:], mu[:, 0:1],
               rstd[:, 0:1], ALU.subtract, ALU.mult, deps=[t8, t3])
    mk._ln_prev = dict(t4=t4, t7=t7, t9=t9)
    return t9


def convert_ffn_weights(mk):
    cfg = mk.cfg
    pr = mk.pr
    iog = pr.cvt
    D, HF, E = cfg["D"], cfg["HF"], cfg["E"]
    DC, HC = D // 128, HF // 128
    Wg_s, Wu_s, Wd_s, Wg_e, Wu_e, Wd_e = mk.ffn_w
    from contextlib import ExitStack
    mk.cvt_es = ExitStack()
    old_es = mk.es_local
    mk.es_local = mk.cvt_es
    h0 = (HC + 1) // 2
    stg = [mk.sb([128, max(h0 * D, DC * h0 * 128)], BF16) for _ in range(2)]
    mk.es_local = old_es
    stg_free = [None, None]
    jobs = []
    for ei in range(E + 1):
        ws = (Wg_s, Wu_s, Wd_s) if ei == 0 else \
            (Wg_e[ei - 1], Wu_e[ei - 1], Wd_e[ei - 1])
        for kind, W in zip("gud", ws):
            jobs += [(kind, ei, W, 0), (kind, ei, W, 1)]
    for ji, (kind, ei, W, half) in enumerate(jobs):
        b = ji % 2
        lo = 0 if half == 0 else h0
        hi = h0 if half == 0 else HC
        nh = hi - lo
        if kind == "d":
            iog.dma(stg[b][:, 0:nh * D],
                    W.rearrange("(c p) n -> p c n", p=128)[:, lo:hi, :],
                    deps=[stg_free[b]])
        else:
            iog.dma(stg[b][:, 0:DC * nh * 128]
                    .rearrange("p (c n) -> p c n", n=nh * 128),
                    W.rearrange("(c p) n -> p c n", p=128)
                    [:, :, lo * 128:hi * 128],
                    deps=[stg_free[b]])
        rd = iog.gtok()
        if kind == "d":
            iog.dma(mk.wd_bf[ei][:, lo * D:hi * D], stg[b][:, 0:nh * D],
                    deps=[rd])
        else:
            off = 0 if kind == "g" else D
            for hc in range(lo, hi):
                iog.dma(mk.wgu_bf[ei, hc, :, off:off + D]
                        .rearrange("p (c f) -> p c f", f=128),
                        stg[b][:, 0:DC * nh * 128]
                        .rearrange("p (c n) -> p c n", n=nh * 128)
                        [:, :, (hc - lo) * 128:(hc - lo + 1) * 128],
                        deps=[rd] if hc == lo else [])
        stg_free[b] = iog.gtok()
    mk.cvt_done = [stg_free[0], stg_free[1]]


def ln_transpose_stream(mk, src_dram, xT_s, S, D, eps, with_convert=False):
    pr = mk.pr
    v, t, io, iog, ios = pr.v, pr.t, pr.io, pr.iog, pr.ios
    NT, DC = S // 128, D // 128
    with mk.scope():
        xt = [mk.sb([128, D], F32) for _ in range(2)]
        xln = [mk.sb([128, D], BF16) for _ in range(2)]
        stage = [mk.sb([128, D], BF16) for _ in range(2)]
        ps_tr = mk.ps([128, 2048], BF16)
        tr_free = [None, None]
        xt_free = [None, None]
        st_free = [None, None]
        for i in range(NT):
            b = i % 2
            io.dma(xt[b][:], src_dram[i * 128:(i + 1) * 128, :],
                   deps=[xt_free[b]])
            dl = io.gtok()
            tln = ln_apply(mk, xt[b], xln[b][:], D, eps, deps=[dl, xt_free[b]])
            xt_free[b] = tln
            evs = []
            for c in range(DC):
                sl = c % 2
                tt = t.run("transpose", ps_tr[:, sl * 1024:sl * 1024 + 128],
                           xln[b][:, c * 128:(c + 1) * 128], mk.identb_sb[:],
                           deps=[tln, tr_free[sl], st_free[b], mk.identb_tok])
                ev = v.run("tensor_copy", stage[b][:, c * 128:(c + 1) * 128],
                           ps_tr[:, sl * 1024:sl * 1024 + 128], deps=[tt])
                tr_free[sl] = ev
                evs.append(ev)
            ios.dma(xT_s[i], stage[b][:], deps=evs)
            st_free[b] = ios.gtok()


def build_attn_a(mk, x_in, Wqkv, Wo, xT_s, q_s, k_s, v_s, o_s, x1_s):
    cfg = mk.cfg
    S, D, HD, W = cfg["S"], cfg["D"], cfg["HD"], cfg["W"]
    scale_att = 1.0 / math.sqrt(HD)

    stop = getattr(mk, "stop", None)
    ln_transpose_stream(mk, x_in, xT_s, S, D, 1e-5)
    if stop == "xT":
        return
    v_pass(mk, Wqkv, xT_s, v_s)
    if stop == "v":
        return
    qk_pass(mk, Wqkv, 0, mk.qnw_sb, mk.qnw_tok, q_s, xT_s)
    if stop == "q":
        return
    qk_pass(mk, Wqkv, D, mk.knw_sb, mk.knw_tok, k_s, xT_s)
    convert_ffn_weights(mk)
    if stop == "k":
        return
    attention_pass(mk, q_s, k_s, v_s, o_s, scale_att)
    if stop == "attn":
        return
    wo_residual_pass(mk, Wo, o_s, x_in, x1_s)


def v_pass(mk, Wqkv, xT_s, v_s):
    cfg = mk.cfg
    pr = mk.pr
    v, t, io, iog, ios = pr.v, pr.t, pr.io, pr.iog, pr.ios
    S, D = cfg["S"], cfg["D"]
    NT, DC = S // 128, D // 128
    ND4 = max(1, D // 512)
    NW = min(512, D)
    with mk.scope():
        wv = mk.sb([128, DC * D], BF16, "wv")
        iog.dma(wv[:].rearrange("p (c n) -> p c n", n=D),
                Wqkv.rearrange("(c p) m -> p c m", p=128)[:, :, 2 * D:3 * D])
        dwv = iog.gtok()
        xblk = [mk.sb([128, D], BF16) for _ in range(2)]
        vstage = [mk.sb([128, D], BF16) for _ in range(2)]
        ps_v = [mk.ps([128, ND4 * 512], F32) for _ in range(2)]
        xb_free = [None, None]
        vs_free = [None, None]
        psv_free = [None, None]
        for i in range(NT):
            b = i % 2
            io.dma(xblk[b][:], xT_s[i], deps=[xb_free[b]])
            dxb = io.gtok()
            mm = None
            for c in range(DC):
                for n in range(ND4):
                    mm = t.run(
                        "matmul", ps_v[b][:, n * 512:n * 512 + NW],
                        xblk[b][:, c * 128:(c + 1) * 128],
                        wv[:, c * D + n * NW: c * D + (n + 1) * NW],
                        start=(c == 0), stop=(c == DC - 1),
                        deps=([dxb, dwv, vs_free[b], psv_free[b]]
                              if c == 0 and n == 0 else []))
            evs = []
            for n in range(ND4):
                evs.append(v.run("tensor_copy",
                                 vstage[b][:, n * NW:(n + 1) * NW],
                                 ps_v[b][:, n * 512:n * 512 + NW], deps=[mm]))
            xb_free[b] = evs[-1]
            psv_free[b] = evs[-1]
            ios.dma(v_s[i * 128:(i + 1) * 128, :], vstage[b][:], deps=evs)
            vs_free[b] = ios.gtok()


def qk_pass(mk, Wqkv, wcol0, nw_sb, nw_tok, dst, xT_s):
    cfg = mk.cfg
    pr = mk.pr
    v, s, t, io, iog, ios = pr.v, pr.s, pr.t, pr.io, pr.iog, pr.ios
    S, D, NH, HD = cfg["S"], cfg["D"], cfg["NH"], cfg["HD"]
    DC = D // 128
    TB = min(512, S)
    NTT = TB // 128
    NBK = S // TB
    d2 = HD // 2
    eps_rms = 1e-6
    with mk.scope():
        cos_sb = [mk.sb([128, TB], F32) for _ in range(2)]
        sin_sb = [mk.sb([128, TB], F32) for _ in range(2)]
        cs_free = [[None, None], [None, None]]
        wq = mk.sb([128, DC * D], BF16, "wq")
        iog.dma(wq[:].rearrange("p (c n) -> p c n", n=D),
                Wqkv.rearrange("(c p) m -> p c m", p=128)
                [:, :, wcol0:wcol0 + D])
        dwq = iog.gtok()
        xblk = [mk.sb([128, DC * TB], BF16) for _ in range(2)]
        qblk = mk.sb([128, NH * TB], BF16)
        NSQ = min(2, NH)
        q2 = mk.sb([128, NSQ * TB], F32)
        onescol = mk.sb([1, 128], F32)
        onesrow = mk.sb([128, 1], F32)
        c1 = v.run("memset", onescol[:], 1.0)
        c2 = v.run("memset", onesrow[:], 1.0)
        rstd_v = mk.sb([1, TB], F32)
        stage2 = [mk.sb([128, NH * TB], BF16) for _ in range(2)]
        tmpA2 = [mk.sb([64, TB], F32) for _ in range(2)]
        tmpB2 = [mk.sb([64, TB], F32) for _ in range(2)]
        ps_q = [mk.ps([128, 512], F32) for _ in range(2)]
        ps_ss = mk.ps([128, 512], F32)
        ps_bc = mk.ps([128, 512], F32)
        xb_free = [None, None]
        q_free2 = [None, None]
        st_free2 = [None, None]
        ssq_free = None
        bc_free = None
        rt_free = [None, None]
        rope_prev = [None] * NH
        for i in range(NBK):
            b = i % 2
            io.dma(cos_sb[b][:], mk.cos_d[:, i * TB:(i + 1) * TB],
                   deps=[f for f in cs_free[b] if f is not None])
            io.dma(sin_sb[b][:], mk.sin_d[:, i * TB:(i + 1) * TB])
            cs_tok = io.gtok()
            for r in range(NTT):
                io.dma(xblk[b][:].rearrange("p (c f) -> p c f", f=TB)
                       [:, :, r * 128:(r + 1) * 128],
                       xT_s[i * NTT + r].rearrange("p (c f) -> p c f", f=128),
                       deps=[xb_free[b]] if r == 0 else [])
            dxb = io.gtok()
            ev_q, ss_toks, sq_toks = [], [], []
            last_qmm = None
            mm_ss = None
            for h in range(NH):
                hp = h % 2
                sl = h % NSQ
                mm = None
                for c in range(DC):
                    mm = t.run("matmul", ps_q[hp][:, 0:TB],
                               wq[:, c * D + h * HD: c * D + h * HD + HD],
                               xblk[b][:, c * TB:(c + 1) * TB],
                               start=(c == 0), stop=(c == DC - 1),
                               deps=([dxb, dwq] + (q_free2[hp] or [])
                                     if c == 0 else []))
                last_qmm = mm
                ev = v.run("tensor_copy", qblk[:, h * TB:(h + 1) * TB],
                           ps_q[hp][:, 0:TB],
                           deps=[mm, rope_prev[h]])
                sq = s.run("activation", q2[:, sl * TB:(sl + 1) * TB],
                           qblk[:, h * TB:(h + 1) * TB], AF.Square,
                           deps=[ev, ssq_free if h < NSQ
                                 else ss_toks[h - NSQ]])
                ev_q.append(ev)
                sq_toks.append(sq)
                if h >= 1:
                    ph = h - 1
                    sl_p = ph % NSQ
                    mm_ss = t.run("matmul", ps_ss[0:1, 0:TB],
                                  onesrow[:, 0:1],
                                  q2[:, sl_p * TB:(sl_p + 1) * TB],
                                  start=(ph == 0), stop=False,
                                  deps=([sq_toks[ph], c2, ssq_free]
                                        if ph == 0 else [sq_toks[ph]]))
                    ss_toks.append(mm_ss)
                q_free2[hp] = [ev, sq]
            mm_ss = t.run("matmul", ps_ss[0:1, 0:TB], onesrow[:, 0:1],
                          q2[:, (NH - 1) % NSQ * TB:
                             ((NH - 1) % NSQ + 1) * TB],
                          start=False, stop=True, deps=[sq_toks[NH - 1]])
            ss_toks.append(mm_ss)
            t1 = v.run("tensor_scalar", rstd_v[:], ps_ss[0:1, 0:TB], 1.0 / D,
                       eps_rms, ALU.mult, ALU.add, deps=[mm_ss])
            ssq_free = t1
            t2 = s.run("activation", rstd_v[:], rstd_v[:], AF.Ln, deps=[t1])
            t3 = s.run("activation", rstd_v[:], rstd_v[:], AF.Exp,
                       scale=mk.const(-0.5)[0:1, 0:1],
                       deps=[t2, mk.ctok(-0.5)])
            bc = t.run("matmul", ps_bc[:, 0:TB], onescol[0:1, :], rstd_v[:],
                       start=True, stop=True, deps=[t3, c1, bc_free])
            rp_toks = []
            for h in range(NH):
                nq = v.run("scalar_tensor_tensor",
                           qblk[:, h * TB:(h + 1) * TB],
                           qblk[:, h * TB:(h + 1) * TB],
                           nw_sb[:, h:h + 1], ps_bc[:, 0:TB],
                           ALU.mult, ALU.mult,
                           deps=[bc, ev_q[h], nw_tok])
                bc_free = nq
                hp2 = h % 2
                qq = v if hp2 == 0 else pr.g
                tmpA = tmpA2[hp2]
                tmpB = tmpB2[hp2]
                x1a = qblk[0:d2, h * TB:(h + 1) * TB]
                x2a = qblk[d2:HD, h * TB:(h + 1) * TB]
                cos0 = cos_sb[b][0:d2, 0:TB]
                sin0 = sin_sb[b][0:d2, 0:TB]
                cos64 = cos_sb[b][64:64 + d2, 0:TB]
                sin64 = sin_sb[b][64:64 + d2, 0:TB]
                o1 = stage2[b][0:d2, h * TB:(h + 1) * TB]
                o2 = stage2[b][d2:HD, h * TB:(h + 1) * TB]
                a1 = qq.run("tensor_tensor", tmpA[:, 0:TB], x1a, cos0,
                            ALU.mult, deps=[nq, cs_tok, rt_free[hp2]])
                a2 = qq.run("tensor_tensor", tmpB[:, 0:TB], x2a, sin64,
                            ALU.mult, deps=[nq])
                a3 = qq.run("tensor_tensor", o1, tmpA[:, 0:TB],
                            tmpB[:, 0:TB], ALU.subtract,
                            deps=[a1, a2, st_free2[b]])
                b1 = qq.run("tensor_tensor", tmpA[:, 0:TB], x2a, cos64,
                            ALU.mult, deps=[a3])
                b2 = qq.run("tensor_tensor", tmpB[:, 0:TB], x1a, sin0,
                            ALU.mult, deps=[a3])
                b3 = qq.run("tensor_tensor", o2, tmpA[:, 0:TB],
                            tmpB[:, 0:TB], ALU.add, deps=[b1, b2, st_free2[b]])
                rt_free[hp2] = b3
                cs_free[b][hp2] = b3
                rope_prev[h] = b3
                rp_toks.append(b3)
            xb_free[b] = last_qmm
            for r in range(NTT):
                ios.dma(dst[i * NTT + r].rearrange("p (hh f) -> p hh f",
                                                   f=128),
                        stage2[b][:].rearrange("p (hh f) -> p hh f", f=TB)
                        [:, :, r * 128:(r + 1) * 128],
                        deps=rp_toks if r == 0 else [])
            st_free2[b] = ios.gtok()


def attention_pass(mk, q_s, k_s, v_s, o_s, scale_att):
    cfg = mk.cfg
    pr = mk.pr
    v, s, t, io, iog, ios = pr.v, pr.s, pr.t, pr.io, pr.iog, pr.ios
    S, D, NH, HD, W = cfg["S"], cfg["D"], cfg["NH"], cfg["HD"], cfg["W"]
    NT, KW = S // 128, W // 128
    NSLOT = KW + 2
    with mk.scope():
        qt_sb = [mk.sb([128, NH * 128], BF16) for _ in range(2)]
        kring = [mk.sb([128, NH * 128], BF16) for _ in range(NSLOT)]
        vring = [mk.sb([128, D], BF16) for _ in range(NSLOT)]
        P_sb2 = [mk.sb([128, (KW + 1) * 128], BF16) for _ in range(2)]
        PT_sb = mk.sb([128, (KW + 1) * 128], BF16)
        ostage = [mk.sb([128, NH * 128], BF16) for _ in range(2)]
        den = mk.sb([128, 1], F32)
        mxn = mk.sb([128, 1], F32)
        rec = mk.sb([128, 1], F32)
        ps_sc = mk.ps([128, (KW + 1) * 128], F32)
        ps_tr = mk.ps([128, 2048], BF16)
        ps_o = mk.ps([128, 512], F32)
        kv_last = [None] * NSLOT
        q_free = [None, None]
        o_free = [None, None]
        tr2_free = [None, None]
        P_free2 = [None, None]
        t4_prev = None
        sc_free = None
        o_ps_free = None
        csc = mk.const(scale_att)
        csc_tok = mk.ctok(scale_att)
        ev_o = None
        for qt in range(NT):
            b = qt % 2
            sl = qt % NSLOT
            io.dma(qt_sb[b][:], q_s[qt], deps=[q_free[b]])
            io.dma(kring[sl][:], k_s[qt], deps=[kv_last[sl]])
            io.dma(vring[sl][:], v_s[qt * 128:(qt + 1) * 128, :],
                   deps=[kv_last[sl]])
            dload = io.gtok()
            k0 = max(0, qt - KW)
            nk = qt - k0 + 1

            def emit_tail(hh, t5_):
                nonlocal o_ps_free, ev_o
                b2h = hh % 2
                evs = []
                for j in range(nk):
                    sl2 = j % 2
                    tt = t.run("transpose",
                               ps_tr[:, sl2 * 1024:sl2 * 1024 + 128],
                               P_sb2[b2h][:, j * 128:(j + 1) * 128],
                               mk.identb_sb[:],
                               deps=[t5_, tr2_free[sl2], mk.identb_tok])
                    ev = v.run("tensor_copy",
                               PT_sb[:, j * 128:(j + 1) * 128],
                               ps_tr[:, sl2 * 1024:sl2 * 1024 + 128],
                               deps=[tt])
                    tr2_free[sl2] = ev
                    evs.append(ev)
                P_free2[b2h] = evs[-1]
                mm2 = None
                for j in range(nk):
                    kt = k0 + j
                    ksl = kt % NSLOT
                    mm2 = t.run("matmul", ps_o[:, 0:128],
                                vring[ksl][:, hh * HD:(hh + 1) * HD],
                                PT_sb[:, j * 128:(j + 1) * 128],
                                start=(j == 0), stop=(j == nk - 1),
                                deps=([evs[j], o_ps_free] if j == 0
                                      else [evs[j]]))
                    kv_last[ksl] = mm2
                ev_o = v.run("tensor_copy",
                             ostage[b][:, hh * 128:(hh + 1) * 128],
                             ps_o[:, 0:128], deps=[mm2, o_free[b]])
                o_ps_free = ev_o

            pend = None
            for h in range(NH):
                bank_ops = {}
                for j in range(nk):
                    bank_ops.setdefault(j // 4, []).append(("s", j))
                if nk == KW + 1:
                    bank_ops[0].append(("m0", 0))
                bank_ops[(nk - 1) // 4].append(("m8", nk - 1))
                mm = None
                for bank in sorted(bank_ops):
                    ops = bank_ops[bank]
                    for oi, (kind, j) in enumerate(ops):
                        st_ = (oi == 0)
                        sp_ = (oi == len(ops) - 1)
                        dd = []
                        if bank == 0 and oi == 0:
                            dd = [dload, sc_free]
                        if kind == "s":
                            kt = k0 + j
                            ksl = kt % NSLOT
                            mm = t.run(
                                "matmul", ps_sc[:, j * 128:(j + 1) * 128],
                                qt_sb[b][:, h * 128:(h + 1) * 128],
                                kring[ksl][:, h * 128:(h + 1) * 128],
                                start=st_, stop=sp_, deps=dd)
                            kv_last[ksl] = mm
                        else:
                            msk = mk.m0_sb if kind == "m0" else mk.m8_sb
                            mm = t.run(
                                "matmul", ps_sc[:, j * 128:(j + 1) * 128],
                                mk.identb_sb[:], msk[:],
                                start=st_, stop=sp_,
                                deps=dd + [mk.identb_tok])
                t1 = v.run("reduce_max", mxn[:], ps_sc[:, 0:nk * 128],
                           axis=AXX, deps=[mm])
                t2 = v.run("tensor_scalar", mxn[:], mxn[:], -scale_att, None,
                           ALU.mult, deps=[t1])
                b2 = h % 2
                t3 = s.run("activation", P_sb2[b2][:, 0:nk * 128],
                           ps_sc[:, 0:nk * 128], AF.Exp,
                           scale=csc, bias=mxn[:, 0:1], accum_out=den[:],
                           deps=[t2, P_free2[b2], csc_tok, t4_prev])
                sc_free = t3
                t4 = v.run("reciprocal", rec[:], den[:], deps=[t3])
                t4_prev = t4
                t5 = v.run("tensor_scalar", P_sb2[b2][:, 0:nk * 128],
                           P_sb2[b2][:, 0:nk * 128], rec[:, 0:1], None,
                           ALU.mult, deps=[t4])
                if pend is not None:
                    emit_tail(*pend)
                pend = (h, t5)
            emit_tail(*pend)
            q_free[b] = ev_o
            ios.dma(o_s[qt], ostage[b][:], deps=[ev_o])
            o_free[b] = ios.gtok()


def wo_residual_pass(mk, Wo, o_s, x_in, x1_s):
    cfg = mk.cfg
    pr = mk.pr
    v, t, io, iog, ios = pr.v, pr.t, pr.io, pr.iog, pr.ios
    S, D = cfg["S"], cfg["D"]
    DC = D // 128
    TB = min(512, S)
    NBK = S // TB
    NTT = TB // 128
    with mk.scope():
        wo = mk.sb([128, DC * D], BF16, "wo")
        iog.dma(wo[:].rearrange("p (c n) -> p c n", n=D),
                Wo.rearrange("(c p) m -> p c m", p=128))
        dwo = iog.gtok()
        oblk = [mk.sb([128, DC * TB], BF16) for _ in range(2)]
        yfm = mk.sb([128, DC * TB], BF16)
        xres = [mk.sb([128, D], F32) for _ in range(NTT)]
        x1t = xres
        ps_y = [mk.ps([128, 512], F32) for _ in range(2)]
        ps_tr = mk.ps([128, 2048], BF16)
        ob_free = [None, None]
        y_free = [None, None]
        tr_free = [None, None]
        x1_free = [None] * NTT
        for i in range(NBK):
            b = i % 2
            for r in range(NTT):
                io.dma(oblk[b][:].rearrange("p (c f) -> p c f", f=TB)
                       [:, :, r * 128:(r + 1) * 128],
                       o_s[i * NTT + r].rearrange("p (c f) -> p c f", f=128),
                       deps=[ob_free[b]] if r == 0 else [])
            for j in range(NTT):
                io.dma(xres[j][:],
                       x_in[i * TB + j * 128: i * TB + (j + 1) * 128, :],
                       deps=[x1_free[j]])
            dob = io.gtok()
            evs = []
            for dc in range(DC):
                dp = dc % 2
                mm = None
                for c in range(DC):
                    mm = t.run("matmul", ps_y[dp][:, 0:TB],
                               wo[:, c * D + dc * 128: c * D + (dc + 1) * 128],
                               oblk[b][:, c * TB:(c + 1) * TB],
                               start=(c == 0), stop=(c == DC - 1),
                               deps=([dob, dwo, y_free[dp]] if c == 0 else []))
                ev = v.run("tensor_copy", yfm[:, dc * TB:(dc + 1) * TB],
                           ps_y[dp][:, 0:TB], deps=[mm])
                y_free[dp] = ev
                evs.append(ev)
            ob_free[b] = evs[-1]
            for j in range(NTT):
                don = []
                for dc in range(DC):
                    sl = dc % 2
                    tt = t.run("transpose",
                               ps_tr[:, sl * 1024:sl * 1024 + 128],
                               yfm[:, dc * TB + j * 128:
                                   dc * TB + (j + 1) * 128],
                               mk.identb_sb[:], deps=[evs[dc], tr_free[sl]])
                    ad = v.run("tensor_tensor",
                               x1t[j][:, dc * 128:(dc + 1) * 128],
                               ps_tr[:, sl * 1024:sl * 1024 + 128],
                               xres[j][:, dc * 128:(dc + 1) * 128], ALU.add,
                               deps=[tt, dob, x1_free[j]])
                    tr_free[sl] = ad
                    don.append(ad)
                ios.dma(x1_s[i * TB + j * 128: i * TB + (j + 1) * 128, :],
                        x1t[j][:], deps=don)
                x1_free[j] = ios.gtok()




F32 = mybir.dt.float32
BF16 = mybir.dt.bfloat16
AF = mybir.ActivationFunctionType
ALU = mybir.AluOpType
AXX = mybir.AxisListType.X


def build_ffn(mk, src_dram, dst_dram, T, Wg_s, Wu_s, Wd_s, Wg_e, Wu_e, Wd_e,
              Wr, br_d, rb_d, final):
    cfg = mk.cfg
    pr = mk.pr
    v, s, t, io, iog, ios = pr.v, pr.s, pr.t, pr.io, pr.iog, pr.ios
    D, HF, E = cfg["D"], cfg["HF"], cfg["E"]
    DC, HC = D // 128, HF // 128
    TB = min(1024, T)
    NBK = T // TB
    NTT = TB // 128
    ND4 = max(1, D // 512)
    NW = min(512, D)
    TW = min(512, TB)
    NC5 = TB // TW
    tpc = TW // 128

    if getattr(mk, "cvt_es", None) is not None:
        for q in pr.barrier_queues():
            q._barrier_deps = (q._barrier_deps or []) +                 list(getattr(mk, "cvt_done", []))
        mk.cvt_es.close()
        mk.cvt_es = None
    with mk.scope():
        wr_sb = mk.sb([128, DC * E], BF16)
        iog.dma(wr_sb[:].rearrange("p (c n) -> p c n", n=E),
                Wr.rearrange("(c p) m -> p c m", p=128))
        dwr = iog.gtok()
        brrow = mk.sb([1, E], F32)
        rbrow = mk.sb([1, E], F32)
        io.dma(brrow[:], br_d)
        io.dma(rbrow[:], rb_d)
        dbr = io.gtok()
        onescol = mk.sb([1, 128], F32)
        c1 = v.run("memset", onescol[:], 1.0)
        bias_r = mk.sb([1, E], F32)
        tb1 = v.run("tensor_tensor", bias_r[:], brrow[:], rbrow[:], ALU.add,
                    deps=[dbr])
        ps_y = [mk.ps([128, 512], F32) for _ in range(2)]
        ps_r = ps_y[0]
        bcmm = t.run("matmul", ps_r[:, 0:E], onescol[0:1, :], bias_r[:],
                     start=True, stop=True, deps=[tb1, c1])
        bias_bc = mk.sb([128, E], F32)
        tb2 = v.run("tensor_copy", bias_bc[:], ps_r[:, 0:E], deps=[bcmm])
        ones8 = mk.sb([128, E], F32)
        c8 = v.run("memset", ones8[:], 1.0)

        x2acc = [mk.sb([128, D], F32) for _ in range(NTT)]
        xln = mk.sb([128, D], BF16)
        xLT = mk.sb([128, DC * TB], BF16)
        h_e = mk.sb([128, HC * TB], BF16)
        w_rt = [mk.sb([128, E], F32) for _ in range(NTT)]
        sg2 = [mk.sb([128, 512], BF16) for _ in range(2)]
        wgu = [mk.sb([128, 2 * D], BF16) for _ in range(2)]
        wd = [mk.sb([128, HC * D], BF16)]
        ps_tr = mk.ps([128, 2048], BF16)
        ps_g = [mk.ps([128, 512], F32) for _ in range(2)]
        ps_u = [mk.ps([128, 512], F32) for _ in range(2)]

        x2_free = [None] * NTT
        xln_free = None
        xLT_free = None
        h_free = None
        wgu_free = [None, None]
        wd_free = [None, None]
        tr_free = [None, None]
        sg_free = [None, None]
        yps_free = [None, None]
        gps_free = [None, None]
        ups_free = [None, None]
        rps_free = None
        w_free = [None] * NTT

        wsrc = [(None, Wg_s, Wu_s, Wd_s)] + [
            (e, Wg_e, Wu_e, Wd_e) for e in range(E)]

        for i in range(NBK):
            ln_toks = []
            for j in range(NTT):
                io.dma(x2acc[j][:],
                       src_dram[i * TB + j * 128: i * TB + (j + 1) * 128, :],
                       deps=[x2_free[j]])
                dx = io.gtok()
                tln = ln_apply(mk, x2acc[j], xln[:], D, 1e-5,
                               deps=[dx, xln_free])
                evs = []
                for c in range(DC):
                    sl = c % 2
                    tt = t.run("transpose",
                               ps_tr[:, sl * 1024:sl * 1024 + 128],
                               xln[:, c * 128:(c + 1) * 128], mk.identb_sb[:],
                               deps=[tln, tr_free[sl], xLT_free,
                                     mk.identb_tok])
                    ev = v.run("tensor_copy",
                               xLT[:, c * TB + j * 128: c * TB + (j + 1) * 128],
                               ps_tr[:, sl * 1024:sl * 1024 + 128], deps=[tt])
                    tr_free[sl] = ev
                    evs.append(ev)
                xln_free = evs[-1]
                ln_toks.append(evs)
                mm = None
                for c in range(DC):
                    mm = t.run("matmul", ps_r[:, 0:E],
                               xLT[:, c * TB + j * 128: c * TB + (j + 1) * 128],
                               wr_sb[:, c * E:(c + 1) * E],
                               start=(c == 0), stop=(c == DC - 1),
                               deps=[evs[c]] + ([dwr, rps_free,
                                                 yps_free[0]]
                                                if c == 0 else []))
                lg = mk.sb([128, E], F32)
                r1 = v.run("tensor_tensor", lg[:], ps_r[:, 0:E], bias_bc[:],
                           ALU.add, deps=[mm, tb2])
                rps_free = r1
                yps_free[0] = r1
                pb = mk.sb([128, E], F32)
                r2 = s.run("activation", pb[:], lg[:], AF.Sigmoid, deps=[r1])
                v1 = mk.sb([128, 1], F32)
                v2 = mk.sb([128, 1], F32)
                m1 = mk.sb([128, E], F32)
                m2 = mk.sb([128, E], F32)
                pm = mk.sb([128, E], F32)
                r3 = v.run("reduce_max", v1[:], pb[:], axis=AXX, deps=[r2])
                r4 = v.run("tensor_scalar", m1[:], pb[:], v1[:, 0:1], None,
                           ALU.is_equal, deps=[r3])
                r5 = v.run("scalar_tensor_tensor", pm[:], m1[:], -1e9, pb[:],
                           ALU.mult, ALU.add, deps=[r4])
                r6 = v.run("reduce_max", v2[:], pm[:], axis=AXX, deps=[r5])
                r7 = v.run("tensor_scalar", m2[:], pm[:], v2[:, 0:1], None,
                           ALU.is_equal, deps=[r6])
                s12 = mk.sb([128, 1], F32)
                r8 = v.run("tensor_tensor", s12[:], v1[:], v2[:], ALU.add,
                           deps=[r3, r6])
                r9 = v.run("reciprocal", s12[:], s12[:], deps=[r8])
                r10 = v.run("tensor_scalar", m1[:], m1[:], v1[:, 0:1], None,
                            ALU.mult, deps=[r4, r5, w_free[j]])
                r11 = v.run("scalar_tensor_tensor", w_rt[j][:], m2[:],
                            v2[:, 0:1], m1[:], ALU.mult, ALU.add,
                            deps=[r7, r10])
                r12 = v.run("tensor_scalar", w_rt[j][:], w_rt[j][:],
                            s12[:, 0:1], None, ALU.mult, deps=[r9, r11])
                w_free[j] = r12
            xLT_free = None
            for ei, (e, Wg, Wu, Wd) in enumerate(wsrc):
                wb = 0
                wgb = ei % 2
                dwd_prev = wd_free[wb]
                io.dma(wd[wb][:], mk.wd_bf[ei],
                       deps=[dwd_prev] + list(getattr(mk, "cvt_done", [])))
                dwd = io.gtok()
                htoks = []
                gp = 0
                for hc in range(HC):
                    io.dma(wgu[wgb][:], mk.wgu_bf[ei, hc],
                           deps=[wgu_free[wgb]] +
                                list(getattr(mk, "cvt_done", [])))
                    dwgu = io.gtok()
                    for tcb in range(NC5):
                        gp ^= 1
                        mmg = mmu = None
                        for c in range(DC):
                            dd = [dwgu, gps_free[gp]] if c == 0 else []
                            mmg = t.run(
                                "matmul", ps_g[gp][:, 0:TW],
                                wgu[wgb][:, c * 128:(c + 1) * 128],
                                xLT[:, c * TB + tcb * TW:
                                    c * TB + (tcb + 1) * TW],
                                start=(c == 0), stop=(c == DC - 1),
                                deps=dd + [ln_toks[min(NTT - 1, (tcb + 1) * tpc - 1)][c]])
                        for c in range(DC):
                            dd = [dwgu, ups_free[gp]] if c == 0 else []
                            mmu = t.run(
                                "matmul", ps_u[gp][:, 0:TW],
                                wgu[wgb][:, D + c * 128: D + (c + 1) * 128],
                                xLT[:, c * TB + tcb * TW:
                                    c * TB + (tcb + 1) * TW],
                                start=(c == 0), stop=(c == DC - 1), deps=dd)
                        sg = sg2[gp]
                        a1 = s.run("activation", sg[:, 0:TW], ps_g[gp][:, 0:TW],
                                   AF.Silu, deps=[mmg, sg_free[gp]])
                        gps_free[gp] = a1
                        a2 = v.run("tensor_tensor",
                                   h_e[:, hc * TB + tcb * TW:
                                       hc * TB + (tcb + 1) * TW],
                                   ps_u[gp][:, 0:TW], sg[:, 0:TW], ALU.mult,
                                   deps=[mmu, a1, h_free])
                        ups_free[gp] = a2
                        sg_free[gp] = a2
                        htoks.append(a2)
                    wgu_free[wgb] = htoks[-1]
                acc_toks = []
                for j in range(NTT):
                    for dc in range(ND4):
                        di = dc % 2
                        mm_d = None
                        for hc in range(HC):
                            dd = []
                            if hc == 0:
                                dd = [dwd, yps_free[di]]
                            mm_d = t.run(
                                "matmul", ps_y[di][:, 0:NW],
                                h_e[:, hc * TB + j * 128:
                                    hc * TB + (j + 1) * 128],
                                wd[wb][:, hc * D + dc * NW:
                                       hc * D + (dc + 1) * NW],
                                start=(hc == 0), stop=(hc == HC - 1),
                                deps=dd + [htoks[hc * NC5 + j // tpc]])
                        if e is None:
                            ac = v.run("tensor_tensor",
                                       x2acc[j][:, dc * NW:(dc + 1) * NW],
                                       ps_y[di][:, 0:NW],
                                       x2acc[j][:, dc * NW:(dc + 1) * NW],
                                       ALU.add, deps=[mm_d, w_free[j]])
                        else:
                            ac = v.run("scalar_tensor_tensor",
                                       x2acc[j][:, dc * NW:(dc + 1) * NW],
                                       ps_y[di][:, 0:NW],
                                       w_rt[j][:, e:e + 1],
                                       x2acc[j][:, dc * NW:(dc + 1) * NW],
                                       ALU.mult, ALU.add,
                                       deps=[mm_d, w_free[j]])
                        acc_toks.append(ac)
                        yps_free[di] = ac
                h_free = acc_toks[-1]
                wd_free[wb] = acc_toks[-1]
            for j in range(NTT):
                iog.dma(dst_dram[i * TB + j * 128: i * TB + (j + 1) * 128, :],
                        x2acc[j][:], deps=[acc_toks[j * ND4 + dc]
                                           for dc in range(ND4)] + [acc_toks[-1]])
                x2_free[j] = iog.gtok()
                if final:
                    mk._final_tokens.append(x2_free[j])
            xLT_free = x2_free[0]





F32 = mybir.dt.float32
I32 = mybir.dt.int32
BF16 = mybir.dt.bfloat16
AF = mybir.ActivationFunctionType
ALU = mybir.AluOpType
AXX = mybir.AxisListType.X
TWO_PI = float(2.0 * math.pi)
HALF_PI = float(0.5 * math.pi)


def build_rglru(mk, x2_s, x3_s, h_s, arp_d, aip_d, Win, bin_d, Wgx, bgx_d,
                Wga, bga_d, Wout, bout_d):
    cfg = mk.cfg
    pr = mk.pr
    v, s, t, io, iog, ios = pr.v, pr.s, pr.t, pr.io, pr.iog, pr.ios
    S, D, NB, DH = cfg["S"], cfg["D"], cfg["NB"], cfg["DH"]
    DC = D // 128
    GC = DH // 128
    CPB = (D // NB) // 128
    assert DH // NB == 128
    TB = min(256, S)
    NBK = S // TB
    NTT = TB // 128

    with mk.scope():
        arp_sb, darp = mk.load_vec(arp_d, GC)
        aip_sb, daip = mk.load_vec(aip_d, GC)
        bin_sb, dbin = mk.load_vec(bin_d, DC)
        bgx_sb, dbgx = mk.load_vec(bgx_d, GC)
        bga_sb, dbga = mk.load_vec(bga_d, GC)
        m8sp = mk.sb([128, GC], F32)
        t_abs = s.run("activation", m8sp[:], arp_sb[:], AF.Abs, deps=[darp])
        e1 = mk.sb([128, GC], F32)
        cm1 = mk.const(-1.0)
        t_e = s.run("activation", e1[:], m8sp[:], AF.Exp, scale=cm1,
                    deps=[t_abs, mk.ctok(-1.0)])
        t_l = v.run("tensor_scalar", e1[:], e1[:], 1.0, None, ALU.add,
                    deps=[t_e])
        t_ln = s.run("activation", e1[:], e1[:], AF.Ln, deps=[t_l])
        t_mx = v.run("tensor_scalar", m8sp[:], arp_sb[:], 0.0, None, ALU.max,
                     deps=[darp, t_abs])
        t_sp = v.run("tensor_tensor", m8sp[:], m8sp[:], e1[:], ALU.add,
                     deps=[t_mx, t_ln])
        t_m8 = v.run("tensor_scalar", m8sp[:], m8sp[:], -8.0, None, ALU.mult,
                     deps=[t_sp])

        win_sb = mk.sb([128, DC * D], BF16)
        iog.dma(win_sb[:].rearrange("p (c n) -> p c n", n=D),
                Win.rearrange("(c p) m -> p c m", p=128))
        dwin = iog.gtok()
        bpc = D // NB // 128
        wgx_sb = mk.sb([128, NB * bpc * 128], BF16)
        iog.dma(wgx_sb[:].rearrange("p (n c f) -> p n c f", f=128, c=bpc),
                Wgx.rearrange("n (c p) f -> p n c f", p=128))
        wga_sb = mk.sb([128, NB * bpc * 128], BF16)
        iog.dma(wga_sb[:].rearrange("p (n c f) -> p n c f", f=128, c=bpc),
                Wga.rearrange("n (c p) f -> p n c f", p=128))
        dwg = iog.gtok()

        xblk = [mk.sb([128, D], F32) for _ in range(NTT)]
        xln = mk.sb([128, D], BF16)
        xLT = mk.sb([128, DC * TB], BF16)
        xh = mk.sb([128, DC * TB], F32)
        xhb = mk.sb([128, DC * TB], BF16)
        _mkt = lambda dt=F32: [mk.sb([128, TB], dt) for _ in range(2)]
        ga2, la2, th2, mag2 = _mkt(), _mkt(), _mkt(), _mkt()
        TH2, uin2, rr2, rm2 = _mkt(), _mkt(I32), _mkt(), _mkt()
        gx2, nrm2, cs2, sn2 = _mkt(), _mkt(), _mkt(), _mkt()
        xr2_, xi2_, tA2, tB2, gr2, gi2 = (_mkt(), _mkt(), _mkt(), _mkt(),
                                          _mkt(), _mkt())
        hstage = mk.sb([128, DC * TB], BF16)
        ones_t = mk.sb([128, TB], F32)
        c_on = v.run("memset", ones_t[:], 1.0)
        thc = mk.sb([128, GC], F32)
        grc = mk.sb([128, GC], F32)
        gic = mk.sb([128, GC], F32)
        z1 = v.run("memset", thc[:], 0.0)
        z2 = v.run("memset", grc[:], 0.0)
        z3 = v.run("memset", gic[:], 0.0)
        carry_tok = [[z1, z2, z3] for _ in range(GC)]

        ps_tr = mk.ps([128, 2048], BF16)
        ps_h = [mk.ps([128, 512], F32) for _ in range(2)]
        ps_g = [mk.ps([128, 512], F32) for _ in range(4)]

        x_free = [None] * NTT
        xln_free = None
        xLT_free = None
        xh_free = None
        tr_free = [None, None]
        hps_free = [None, None]
        gps_free = [None, None, None, None]
        hst_free = None
        prevs = [dict(e9=None, e2=None, e4=None, s3=None, e10=None, q7=None,
                      b4=None, b5=None) for _ in range(2)]

        NC5 = TB // 512 if TB >= 512 else 1
        TW = min(512, TB)

        for i in range(NBK):
            ln_toks = []
            for j in range(NTT):
                io.dma(xblk[j][:],
                       x2_s[i * TB + j * 128: i * TB + (j + 1) * 128, :],
                       deps=[x_free[j]])
                dx = io.gtok()
                tln = ln_apply(mk, xblk[j], xln[:], D, 1e-5,
                               deps=[dx, xln_free])
                x_free[j] = tln
                evs = []
                for c in range(DC):
                    sl = c % 2
                    tt = t.run("transpose",
                               ps_tr[:, sl * 1024:sl * 1024 + 128],
                               xln[:, c * 128:(c + 1) * 128], mk.identb_sb[:],
                               deps=[tln, tr_free[sl],
                                     xLT_free if (j == 0 and c == 0) else None,
                                     mk.identb_tok])
                    ev = v.run("tensor_copy",
                               xLT[:, c * TB + j * 128:
                                   c * TB + (j + 1) * 128],
                               ps_tr[:, sl * 1024:sl * 1024 + 128], deps=[tt])
                    tr_free[sl] = ev
                    evs.append(ev)
                xln_free = evs[-1]
                ln_toks.append(evs)
            xh_toks = []
            hp = 0
            for hc in range(DC):
                for tcb in range(NC5):
                    hp ^= 1
                    mm = None
                    for c in range(DC):
                        dd = [dwin, hps_free[hp]] if c == 0 else []
                        mm = t.run("matmul", ps_h[hp][:, 0:TW],
                                   win_sb[:, c * D + hc * 128:
                                          c * D + (hc + 1) * 128],
                                   xLT[:, c * TB + tcb * TW:
                                       c * TB + (tcb + 1) * TW],
                                   start=(c == 0), stop=(c == DC - 1),
                                   deps=dd + [ln_toks[min(
                                       NTT - 1, (tcb + 1) * (TW // 128) - 1)][c]])
                    w1 = v.run("tensor_scalar",
                               xh[:, hc * TB + tcb * TW:
                                  hc * TB + (tcb + 1) * TW],
                               ps_h[hp][:, 0:TW], bin_sb[:, hc:hc + 1], None,
                               ALU.add, deps=[mm, dbin, xh_free])
                    w2 = v.run("tensor_copy",
                               xhb[:, hc * TB + tcb * TW:
                                   hc * TB + (tcb + 1) * TW],
                               xh[:, hc * TB + tcb * TW:
                                  hc * TB + (tcb + 1) * TW], deps=[w1])
                    hps_free[hp] = w2
                    xh_toks.append(w2)
            xh_free = xh_toks[-1]
            xLT_free = xh_toks[-1]
            hev_all = []
            for gp2 in range(GC // 2):
                pair = [2 * gp2, 2 * gp2 + 1]
                dep_xh = xh_toks[-1]
                T = []
                for idx, gc in enumerate(pair):
                    par = gc % 2
                    T.append(dict(
                        gc=gc, qq=(v if par == 0 else pr.g),
                        prev=prevs[par],
                        gx=gx2[par], ga=ga2[par], la=la2[par], th=th2[par],
                        mag=mag2[par], nrm=nrm2[par], TH=TH2[par],
                        uin=uin2[par], rr=rr2[par], rm=rm2[par],
                        cs=cs2[par], sn=sn2[par], xr=xr2_[par], xi=xi2_[par],
                        tA=tA2[par], tB=tB2[par], gr=gr2[par], gi=gi2[par],
                        px=2 * idx, pa=2 * idx + 1))
                for d in T:
                    gc = d["gc"]
                    mmx = None
                    for c in range(bpc):
                        xi_c = gc * bpc + c
                        dd0 = [dwg, gps_free[d["px"]]] if c == 0 else []
                        mmx = t.run("matmul", ps_g[d["px"]][:, 0:TB],
                                    wgx_sb[:, xi_c * 128:(xi_c + 1) * 128],
                                    xhb[:, xi_c * TB:(xi_c + 1) * TB],
                                    start=(c == 0), stop=(c == bpc - 1),
                                    deps=dd0 + [dep_xh])
                    d["mmx"] = mmx
                    mma = None
                    for c in range(bpc):
                        xi_c = gc * bpc + c
                        dd0 = [gps_free[d["pa"]]] if c == 0 else []
                        mma = t.run("matmul", ps_g[d["pa"]][:, 0:TB],
                                    wga_sb[:, xi_c * 128:(xi_c + 1) * 128],
                                    xhb[:, xi_c * TB:(xi_c + 1) * TB],
                                    start=(c == 0), stop=(c == bpc - 1),
                                    deps=dd0)
                    d["mma"] = mma
                for d in T:
                    gc = d["gc"]
                    a1 = s.run("activation", d["gx"][:],
                               ps_g[d["px"]][:, 0:TB], AF.Sigmoid,
                               bias=bgx_sb[:, gc:gc + 1],
                               deps=[d["mmx"], dbgx, d["prev"]["e9"]])
                    gps_free[d["px"]] = a1
                    a2 = s.run("activation", d["ga"][:],
                               ps_g[d["pa"]][:, 0:TB], AF.Sigmoid,
                               bias=bga_sb[:, gc:gc + 1],
                               deps=[d["mma"], dbga, d["prev"]["e2"]])
                    gps_free[d["pa"]] = a2
                    d["a1"], d["a2"] = a1, a2
                for d in T:
                    gc = d["gc"]
                    d["e1"] = v.run("tensor_scalar", d["la"][:], d["ga"][:],
                                    m8sp[:, gc:gc + 1], None, ALU.mult,
                                    deps=[d["a2"], t_m8, d["prev"]["e4"]])
                    d["e2t"] = v.run("tensor_scalar", d["th"][:], d["ga"][:],
                                     aip_sb[:, gc:gc + 1], None, ALU.mult,
                                     deps=[d["a2"], daip])
                for d in T:
                    d["e3"] = s.run("activation", d["mag"][:], d["la"][:],
                                    AF.Exp, deps=[d["e1"], d["prev"]["s3"]])
                for d in T:
                    d["e4"] = v.run("tensor_tensor", d["nrm"][:], d["mag"][:],
                                    d["mag"][:], ALU.mult,
                                    deps=[d["e3"], d["prev"]["e10"]])
                    d["e5"] = v.run("tensor_scalar", d["nrm"][:], d["nrm"][:],
                                    -1.0, 1.0, ALU.mult, ALU.add,
                                    deps=[d["e4"]])
                for d in T:
                    d["e6a"] = s.run("activation", d["nrm"][:], d["nrm"][:],
                                     AF.Ln, deps=[d["e5"]])
                for d in T:
                    d["e6"] = s.run("activation", d["nrm"][:], d["nrm"][:],
                                    AF.Exp, scale=mk.const(0.5),
                                    deps=[d["e6a"], mk.ctok(0.5)])
                for d in T:
                    gc = d["gc"]
                    qq = d["qq"]
                    d["e7"] = qq.run("tensor_tensor", d["xr"][:], d["gx"][:],
                                     xh[:, gc * TB:(gc + 1) * TB], ALU.mult,
                                     deps=[d["a1"], dep_xh])
                    d["e8"] = qq.run("tensor_tensor", d["xr"][:], d["xr"][:],
                                     d["nrm"][:], ALU.mult,
                                     deps=[d["e7"], d["e6"]])
                    d["e9"] = qq.run("tensor_tensor", d["xi"][:], d["gx"][:],
                                     xh[:, (GC + gc) * TB:(GC + gc + 1) * TB],
                                     ALU.mult, deps=[d["a1"], dep_xh])
                    d["e10"] = qq.run("tensor_tensor", d["xi"][:], d["xi"][:],
                                      d["nrm"][:], ALU.mult,
                                      deps=[d["e9"], d["e6"]])
                for d in T:
                    gc = d["gc"]
                    s1 = v.run("tensor_tensor_scan", d["TH"][:], ones_t[:],
                               d["th"][:], thc[:, gc:gc + 1], ALU.mult,
                               ALU.add, deps=[d["e2t"], c_on,
                                              carry_tok[gc][0]])
                    s1b = v.run("tensor_copy", thc[:, gc:gc + 1],
                                d["TH"][:, TB - 1:TB], deps=[s1])
                    carry_tok[gc][0] = s1b
                    d["s1"] = s1
                for d in T:
                    d["q1"] = v.run("tensor_scalar", d["uin"][:], d["TH"][:],
                                    1.0 / TWO_PI, None, ALU.mult,
                                    deps=[d["s1"]])
                    d["q2"] = v.run("scalar_tensor_tensor", d["rr"][:],
                                    d["uin"][:], -TWO_PI, d["TH"][:],
                                    ALU.mult, ALU.add,
                                    deps=[d["q1"], d["prev"]["q7"]])
                    d["q2a"] = v.run("tensor_scalar", d["rm"][:], d["rr"][:],
                                     math.pi, None, ALU.is_ge,
                                     deps=[d["q2"]])
                    d["q2b"] = v.run("scalar_tensor_tensor", d["rr"][:],
                                     d["rm"][:], -TWO_PI, d["rr"][:],
                                     ALU.mult, ALU.add, deps=[d["q2a"]])
                for d in T:
                    d["q3"] = s.run("activation", d["sn"][:], d["rr"][:],
                                    AF.Sin, deps=[d["q2b"], d["prev"]["b4"]])
                for d in T:
                    d["q4"] = v.run("tensor_scalar", d["rr"][:], d["TH"][:],
                                    HALF_PI, None, ALU.add,
                                    deps=[d["q2b"], d["q3"]])
                    d["q5"] = v.run("tensor_scalar", d["uin"][:], d["rr"][:],
                                    1.0 / TWO_PI, None, ALU.mult,
                                    deps=[d["q4"], d["q2"]])
                    d["q6"] = v.run("scalar_tensor_tensor", d["rr"][:],
                                    d["uin"][:], -TWO_PI, d["rr"][:],
                                    ALU.mult, ALU.add, deps=[d["q5"]])
                    d["q6a"] = v.run("tensor_scalar", d["rm"][:], d["rr"][:],
                                     math.pi, None, ALU.is_ge,
                                     deps=[d["q6"]])
                    d["q6b"] = v.run("scalar_tensor_tensor", d["rr"][:],
                                     d["rm"][:], -TWO_PI, d["rr"][:],
                                     ALU.mult, ALU.add, deps=[d["q6a"]])
                for d in T:
                    d["q7"] = s.run("activation", d["cs"][:], d["rr"][:],
                                    AF.Sin, deps=[d["q6b"], d["prev"]["b5"]])
                for d in T:
                    qq = d["qq"]
                    d["r1"] = qq.run("tensor_tensor", d["tA"][:], d["cs"][:],
                                     d["xr"][:], ALU.mult,
                                     deps=[d["q7"], d["e8"]])
                    d["r2"] = qq.run("tensor_tensor", d["tB"][:], d["sn"][:],
                                     d["xi"][:], ALU.mult,
                                     deps=[d["q3"], d["e10"]])
                    d["r3"] = qq.run("tensor_tensor", d["tA"][:], d["tA"][:],
                                     d["tB"][:], ALU.add,
                                     deps=[d["r1"], d["r2"]])
                    d["r4"] = qq.run("tensor_tensor", d["tB"][:], d["cs"][:],
                                     d["xi"][:], ALU.mult,
                                     deps=[d["q7"], d["e10"], d["r3"]])
                    d["r5"] = qq.run("tensor_tensor", d["xr"][:], d["sn"][:],
                                     d["xr"][:], ALU.mult,
                                     deps=[d["q3"], d["r3"]])
                    d["r6"] = qq.run("tensor_tensor", d["tB"][:], d["tB"][:],
                                     d["xr"][:], ALU.subtract,
                                     deps=[d["r4"], d["r5"]])
                for d in T:
                    gc = d["gc"]
                    s2 = v.run("tensor_tensor_scan", d["gr"][:], d["mag"][:],
                               d["tA"][:], grc[:, gc:gc + 1], ALU.mult,
                               ALU.add, deps=[d["e3"], d["r3"],
                                              carry_tok[gc][1]])
                    s2b = v.run("tensor_copy", grc[:, gc:gc + 1],
                                d["gr"][:, TB - 1:TB], deps=[s2])
                    carry_tok[gc][1] = s2b
                    s3 = v.run("tensor_tensor_scan", d["gi"][:], d["mag"][:],
                               d["tB"][:], gic[:, gc:gc + 1], ALU.mult,
                               ALU.add, deps=[d["e3"], d["r6"],
                                              carry_tok[gc][2]])
                    s3b = v.run("tensor_copy", gic[:, gc:gc + 1],
                                d["gi"][:, TB - 1:TB], deps=[s3])
                    carry_tok[gc][2] = s3b
                    d["s2"], d["s3"] = s2, s3
                for d in T:
                    gc = d["gc"]
                    qq = d["qq"]
                    b1 = qq.run("tensor_tensor", d["tA"][:], d["cs"][:],
                                d["gr"][:], ALU.mult,
                                deps=[d["s2"], d["q7"], d["r6"]])
                    b2 = qq.run("tensor_tensor", d["tB"][:], d["sn"][:],
                                d["gi"][:], ALU.mult,
                                deps=[d["s3"], d["q3"], d["r6"]])
                    b3 = qq.run("tensor_tensor",
                                hstage[:, gc * TB:(gc + 1) * TB],
                                d["tA"][:], d["tB"][:], ALU.subtract,
                                deps=[b1, b2, hst_free])
                    b4 = qq.run("tensor_tensor", d["tA"][:], d["sn"][:],
                                d["gr"][:], ALU.mult,
                                deps=[d["s2"], d["q3"], b3])
                    b5 = qq.run("tensor_tensor", d["tB"][:], d["cs"][:],
                                d["gi"][:], ALU.mult,
                                deps=[d["s3"], d["q7"], b3])
                    b6 = qq.run("tensor_tensor",
                                hstage[:, (GC + gc) * TB:(GC + gc + 1) * TB],
                                d["tA"][:], d["tB"][:], ALU.add,
                                deps=[b4, b5, hst_free])
                    d["prev"].update(e9=d["e9"], e2=d["e2t"], e4=d["e4"],
                                     s3=d["s3"], e10=d["e10"], q7=d["q7"],
                                     b4=b4, b5=b5)
                    hev_all.append(b3)
                    hev_all.append(b6)
            for r in range(NTT):
                ios.dma(h_s[i * NTT + r].rearrange("p (c f) -> p c f", f=128),
                        hstage[:].rearrange("p (c f) -> p c f", f=TB)
                        [:, :, r * 128:(r + 1) * 128],
                        deps=hev_all if r == 0 else [])
            hst_free = ios.gtok()

    wout_half_pass(mk, x2_s, x3_s, h_s, Wout, bout_d)


def wout_half_pass(mk, x2_s, x3_s, h_s, Wout, bout_d):
    cfg = mk.cfg
    pr = mk.pr
    v, t, io, iog, ios = pr.v, pr.t, pr.io, pr.iog, pr.ios
    S, D = cfg["S"], cfg["D"]
    SH = S // 2
    DC = D // 128
    ND4 = max(1, D // 512)
    NW = min(512, D)
    NT2 = SH // 128
    with mk.scope():
        wout_sb = mk.sb([128, DC * D], BF16)
        iog.dma(wout_sb[:].rearrange("p (c n) -> p c n", n=D),
                Wout.rearrange("(c p) m -> p c m", p=128))
        dwo = iog.gtok()
        boutrow = mk.sb([1, D], F32)
        io.dma(boutrow[:], bout_d)
        dbo = io.gtok()
        onescol = mk.sb([1, 128], F32)
        c1 = v.run("memset", onescol[:], 1.0)
        bout_bc = mk.sb([128, D], F32)
        ps_b = mk.ps([128, 512], F32)
        btoks = []
        for n in range(ND4):
            bm = t.run("matmul", ps_b[:, 0:NW], onescol[0:1, :],
                       boutrow[:, n * NW:(n + 1) * NW], start=True, stop=True,
                       deps=[dbo, c1] + ([btoks[-1]] if btoks else []))
            be = v.run("tensor_copy", bout_bc[:, n * NW:(n + 1) * NW],
                       ps_b[:, 0:NW], deps=[bm])
            btoks.append(be)
        hA = mk.sb([128, DC * 128], BF16)
        hB = mk.sb([128, DC * 128], BF16)
        hO = mk.sb([128, DC * 128], BF16)
        x2A = mk.sb([128, D], F32)
        x2B = mk.sb([128, D], F32)
        x3t = mk.sb([128, D], F32)
        ps_y = [mk.ps([128, 512], F32) for _ in range(2)]
        hload_free = None
        x3_free = None
        yps_free = [None, None]
        for i in range(NT2):
            io.dma(hA[:], h_s[i], deps=[hload_free])
            io.dma(hB[:], h_s[SH // 128 + i], deps=[hload_free])
            io.dma(x2A[:], x2_s[i * 128:(i + 1) * 128, :], deps=[x3_free])
            io.dma(x2B[:], x2_s[SH + i * 128: SH + (i + 1) * 128, :],
                   deps=[x3_free])
            dl = io.gtok()
            s1 = v.run("tensor_scalar", hO[:], hA[:],
                       mk.halfsel_sb[:, 0:1], None, ALU.mult,
                       deps=[dl, mk.halfsel_tok, hload_free])
            s2 = v.run("scalar_tensor_tensor", hO[:], hB[:],
                       mk.halfsel_sb[:, 1:2], hO[:], ALU.mult, ALU.add,
                       deps=[s1])
            s3 = v.run("tensor_scalar", x2A[:], x2A[:],
                       mk.halfsel_sb[:, 0:1], None, ALU.mult, deps=[dl])
            s4 = v.run("scalar_tensor_tensor", x2A[:], x2B[:],
                       mk.halfsel_sb[:, 1:2], x2A[:], ALU.mult, ALU.add,
                       deps=[s3])
            hload_free = s2
            ytoks = []
            for n in range(ND4):
                np_ = n % 2
                mm = None
                for c in range(DC):
                    dd = [dwo, s2, yps_free[np_]] if c == 0 else []
                    mm = t.run("matmul", ps_y[np_][:, 0:NW],
                               hO[:, c * 128:(c + 1) * 128],
                               wout_sb[:, c * D + n * NW:
                                       c * D + (n + 1) * NW],
                               start=(c == 0), stop=(c == DC - 1), deps=dd)
                y1 = v.run("scalar_tensor_tensor",
                           x3t[:, n * NW:(n + 1) * NW], ps_y[np_][:, 0:NW],
                           1.0, bout_bc[:, n * NW:(n + 1) * NW], ALU.mult,
                           ALU.add, deps=[mm, btoks[n], x3_free])
                y2 = v.run("tensor_tensor", x3t[:, n * NW:(n + 1) * NW],
                           x3t[:, n * NW:(n + 1) * NW],
                           x2A[:, n * NW:(n + 1) * NW], ALU.add,
                           deps=[y1, s4])
                yps_free[np_] = y2
                ytoks.append(y2)
            ios.dma(x3_s[i * 128:(i + 1) * 128, :], x3t[:], deps=ytoks)
            x3_free = ios.gtok()








F32 = mybir.dt.float32
I32 = mybir.dt.int32
BF16 = mybir.dt.bfloat16
AF = mybir.ActivationFunctionType
ALU = mybir.AluOpType
AXX = mybir.AxisListType.X
TWO_PI = float(2.0 * math.pi)


def make_cfg(S=4096, D=2048, NH=16, HD=128, W=1024, HF=1408, E=8, NB=8,
             theta=10000.0):
    return dict(S=S, D=D, NH=NH, HD=HD, W=W, HF=HF, E=E, NB=NB, DH=D // 2,
                theta=theta)


class MK:
    def __init__(self, cfg, debug_outputs=()):
        self.cfg = cfg
        self.debug_outputs = set(debug_outputs)
        self.nc = bass.Bass("TRN2", target_bir_lowering=False, debug=False)
        self.es = ExitStack()
        self.es_local = self.es
        self.pr = Prog(self.nc, self.es)
        self.dram = {}
        self._n = 0
        self._final_tokens = []

    @contextmanager
    def scope(self):
        old = self.es_local
        self.es_local = ExitStack()
        try:
            yield
        finally:
            self.es_local.close()
            self.es_local = old
            self.pr.barrier()

    def sb(self, shape, dt=F32, name=None):
        self._n += 1
        nm = f"{name}_{self._n}" if name else f"sb{self._n}"
        return self.es_local.enter_context(
            self.nc.sbuf_tensor(nm, shape, dt))

    def ps(self, shape, dt=F32, name=None):
        self._n += 1
        bank = 512 if dt == F32 or dt == I32 else 1024
        shape = list(shape)
        shape[-1] = ((shape[-1] + bank - 1) // bank) * bank
        return self.es_local.enter_context(
            self.nc.psum_tensor(name or f"ps{self._n}", shape, dt))

    def din(self, name, shape, dt=F32):
        ap = self.nc.dram_tensor(name, shape, dt, kind="ExternalInput").ap()
        self.dram[name] = ap
        return ap

    def dout(self, name, shape, dt=F32):
        ap = self.nc.dram_tensor(name, shape, dt, kind="ExternalOutput").ap()
        self.dram[name] = ap
        return ap

    def dscratch(self, name, shape, dt=F32):
        kind = "ExternalOutput" if name in self.debug_outputs else "Internal"
        ap = self.nc.dram_tensor(name, shape, dt, kind=kind).ap()
        self.dram[name] = ap
        return ap

    def setup_consts(self):
        self._cvals = {}
        self._ctile = self.sb([128, 16], F32, "cpool")
        self._cn = 0

    def const(self, val):
        val = float(val)
        if val not in self._cvals:
            i = self._cn
            assert i < 16
            self._cn += 1
            tok = self.pr.v.run("memset", self._ctile[:, i:i + 1], val)
            self._cvals[val] = (self._ctile[:, i:i + 1], tok)
        return self._cvals[val][0]

    def ctok(self, val):
        return self._cvals[float(val)][1]

    def load_vec(self, dram_ap, C, dt=F32):
        t = self.sb([128, C], dt)
        self.pr.io.dma(t[:], dram_ap)
        return t, self.pr.io.gtok()


def build(cfg, debug_outputs=(), stop=None):

    mk = MK(cfg, debug_outputs)
    nc, pr = mk.nc, mk.pr
    S, D, E, HF, HD = cfg["S"], cfg["D"], cfg["E"], cfg["HF"], cfg["HD"]
    SH = S // 2

    x_in = mk.din("x", [S, D], F32)
    Wqkv = mk.din("L0_attn_a_Wqkv", [D, 3 * D], F32)
    qnw_d = mk.din("qnw_t", [128, D // 128], F32)
    knw_d = mk.din("knw_t", [128, D // 128], F32)
    Wo = mk.din("L0_attn_a_Wo", [D, D], F32)
    arp_d = mk.din("arp_t", [128, cfg["DH"] // 128], F32)
    aip_d = mk.din("aip_t", [128, cfg["DH"] // 128], F32)
    Win = mk.din("L0_attn_b_W_in", [D, D], F32)
    bin_d = mk.din("bin_t", [128, D // 128], F32)
    Wgx = mk.din("L0_attn_b_Wg_x", [cfg["NB"], D // cfg["NB"],
                                    cfg["DH"] // cfg["NB"]], F32)
    bgx_d = mk.din("bgx_t", [128, cfg["DH"] // 128], F32)
    Wga = mk.din("L0_attn_b_Wg_a", [cfg["NB"], D // cfg["NB"],
                                    cfg["DH"] // cfg["NB"]], F32)
    bga_d = mk.din("bga_t", [128, cfg["DH"] // 128], F32)
    Wout = mk.din("L0_attn_b_W_out", [D, D], F32)
    bout_d = mk.din("bout_row", [1, D], F32)
    Wg_s = mk.din("L0_ffn_Wg_s", [D, HF], F32)
    Wu_s = mk.din("L0_ffn_Wu_s", [D, HF], F32)
    Wd_s = mk.din("L0_ffn_Wd_s", [HF, D], F32)
    Wg_e = mk.din("L0_ffn_Wg", [E, D, HF], F32)
    Wu_e = mk.din("L0_ffn_Wu", [E, D, HF], F32)
    Wd_e = mk.din("L0_ffn_Wd", [E, HF, D], F32)
    Wr = mk.din("L0_ffn_Wr", [D, E], F32)
    brrb_d = mk.din("brrb_row", [1, E], F32)
    br_d = mk.din("br_row", [1, E], F32)
    rb_d = mk.din("rb_row", [1, E], F32)
    identb = mk.din("identb", [128, 128], BF16)
    m0add = mk.din("m0add", [128, 128], BF16)
    m8add = mk.din("m8add", [128, 128], BF16)
    halfsel = mk.din("halfsel", [128, 2], F32)
    cos_d = mk.din("cos_t", [128, S], F32)
    sin_d = mk.din("sin_t", [128, S], F32)
    out = mk.dout("out", [SH, D], F32)

    NT = S // 128
    xT_s = mk.dscratch("xT_s", [NT, 128, D], BF16)
    q_s = mk.dscratch("q_s", [NT, 128, D], BF16)
    k_s = mk.dscratch("k_s", [NT, 128, D], BF16)
    v_s = mk.dscratch("v_s", [S, D], BF16)
    o_s = mk.dscratch("o_s", [NT, 128, D], BF16)
    x1_s = mk.dscratch("x1_s", [S, D], F32)
    x2_s = mk.dscratch("x2_s", [S, D], F32)
    h_s = mk.dscratch("h_s", [NT, 128, D], BF16)
    x3_s = mk.dscratch("x3_s", [SH, D], F32)
    HCW = HF // 128
    wgu_bf = mk.dscratch("wgu_bf", [E + 1, HCW, 128, 2 * D], BF16)
    wd_bf = mk.dscratch("wd_bf", [E + 1, 128, HCW * D], BF16)
    mk.wgu_bf, mk.wd_bf = wgu_bf, wd_bf
    mk.ffn_w = (Wg_s, Wu_s, Wd_s, Wg_e, Wu_e, Wd_e)

    mk.setup_consts()
    mk._scrap = mk.sb([128, max(D, HF)], F32, "scrap")
    mk._scrap_free = None
    mk.identb_sb = mk.sb([128, 128], BF16, "identb_sb")
    pr.io.dma(mk.identb_sb[:], identb)
    mk.m0_sb = mk.sb([128, 128], BF16, "m0_sb")
    pr.io.dma(mk.m0_sb[:], m0add)
    mk.m8_sb = mk.sb([128, 128], BF16, "m8_sb")
    pr.io.dma(mk.m8_sb[:], m8add)
    gl = pr.io.gtok()
    mk.identb_tok = mk.m0_tok = mk.m8_tok = gl
    mk.cos_d = cos_d
    mk.sin_d = sin_d
    mk.qnw_sb, mk.qnw_tok = mk.load_vec(qnw_d, D // 128)
    mk.knw_sb, mk.knw_tok = mk.load_vec(knw_d, D // 128)
    mk.halfsel_sb, mk.halfsel_tok = mk.load_vec(halfsel, 2)

    mk.stop = stop
    build_attn_a(mk, x_in, Wqkv, Wo, xT_s, q_s, k_s, v_s, o_s, x1_s)
    if stop is None or stop in ("ffn1", "rglru", "ffn2"):
        build_ffn(mk, x1_s, x2_s, S, Wg_s, Wu_s, Wd_s, Wg_e, Wu_e, Wd_e,
                  Wr, br_d, rb_d, final=False)
        if stop is None or stop in ("rglru", "ffn2"):
            build_rglru(mk, x2_s, x3_s, h_s, arp_d, aip_d, Win, bin_d, Wgx,
                        bgx_d, Wga, bga_d, Wout, bout_d)
            if stop is None or stop == "ffn2":
                build_ffn(mk, x3_s, out, SH, Wg_s, Wu_s, Wd_s, Wg_e, Wu_e,
                          Wd_e, Wr, br_d, rb_d, final=True)

    if not mk._final_tokens:
        toks = []
        for q in pr.queues():
            if len(q.sems) > 1:
                for sem, cnt in zip(q.sems, q.counts):
                    if cnt > 0:
                        toks.append((sem, cnt))
            else:
                t_ = q.tok()
                if t_ is not None:
                    toks.append(t_)
        mk._final_tokens = toks
    with nc.Block() as block:
        pr.emit(block, final_waits=mk._final_tokens)
    mk.es.close()
    return mk


def host_consts(cfg):
    S, HD, W, theta = cfg["S"], cfg["HD"], cfg["W"], cfg["theta"]
    import ml_dtypes
    d2 = HD // 2
    inv = (1.0 / theta ** (np.arange(0, HD, 2) / HD)).astype(np.float64)
    pos = np.arange(S, dtype=np.float64)
    fr = pos[None, :] * inv[:, None]
    cos_t = np.cos(fr).astype(np.float32)
    sin_t = np.sin(fr).astype(np.float32)
    reps = 128 // d2
    cos_t = np.concatenate([cos_t] * reps, axis=0)
    sin_t = np.concatenate([sin_t] * reps, axis=0)
    ident = np.eye(128, dtype=ml_dtypes.bfloat16)
    ii = np.arange(128)
    m0 = np.where(ii[None, :] >= ii[:, None] + 1, 0.0, -1e7)
    m8 = np.where(ii[None, :] <= ii[:, None], 0.0, -1e7)
    return dict(identb=ident,
                m0add=m0.astype(ml_dtypes.bfloat16),
                m8add=m8.astype(ml_dtypes.bfloat16),
                cos_t=cos_t, sin_t=sin_t)


def tile_vec(vec, C=None):
    v = np.asarray(vec).reshape(-1)
    n = v.shape[0] // 128
    return np.ascontiguousarray(v.reshape(n, 128).T).astype(np.float32)


def host_inputs(cfg, core, full_inputs):
    fi = full_inputs
    b, half = core // 2, core % 2
    if "L0_attn_a_hidden_states" in fi:
        d = {"x": np.ascontiguousarray(np.asarray(
            fi["L0_attn_a_hidden_states"])[b])}
    else:
        d = {"x": np.asarray(fi["x"])}
    for k in ["L0_attn_a_Wqkv", "L0_attn_a_Wo", "L0_attn_b_W_in",
              "L0_attn_b_Wg_x", "L0_attn_b_Wg_a", "L0_attn_b_W_out",
              "L0_ffn_Wg_s", "L0_ffn_Wu_s", "L0_ffn_Wd_s", "L0_ffn_Wg",
              "L0_ffn_Wu", "L0_ffn_Wd", "L0_ffn_Wr"]:
        d[k] = fi[k]
    d["qnw_t"] = tile_vec(fi["L0_attn_a_q_norm_w"])
    d["knw_t"] = tile_vec(fi["L0_attn_a_k_norm_w"])
    d["arp_t"] = tile_vec(fi["L0_attn_b_a_real_param"])
    d["aip_t"] = tile_vec(fi["L0_attn_b_a_imag_param"])
    d["bin_t"] = tile_vec(fi["L0_attn_b_b_in"])
    d["bgx_t"] = tile_vec(fi["L0_attn_b_bg_x"])
    d["bga_t"] = tile_vec(fi["L0_attn_b_bg_a"])
    d["bout_row"] = np.ascontiguousarray(
        fi["L0_attn_b_b_out"].reshape(1, -1)).astype(np.float32)
    d["br_row"] = np.ascontiguousarray(
        fi["L0_ffn_br"].reshape(1, -1)).astype(np.float32)
    d["rb_row"] = np.ascontiguousarray(
        fi["L0_ffn_rb"].reshape(1, -1)).astype(np.float32)
    d["brrb_row"] = d["br_row"]
    hs = np.zeros((128, 2), np.float32)
    hs[:, half] = 1.0
    d["halfsel"] = hs
    d.update(host_consts(cfg))
    return d


_NCORES = 8


def kernel(**inputs):
    cfg = make_cfg()
    mk = build(cfg)
    names = set(mk.dram.keys())
    in_maps = []
    for c in range(_NCORES):
        im = host_inputs(cfg, c, inputs)
        in_maps.append({k: v for k, v in im.items() if k in names})
    res = run_bass_kernel_spmd(mk.nc, in_maps, core_ids=list(range(_NCORES)))
    S, D = cfg["S"], cfg["D"]
    out = np.empty((_NCORES // 2, S, D), np.float32)
    for c in range(_NCORES):
        b, half = c // 2, c % 2
        out[b, half * (S // 2):(half + 1) * (S // 2), :] = \
            np.asarray(res.results[c]["out"])
    return out
